# version 20
# baseline (speedup 1.0000x reference)
import math
import contextlib
import numpy as np
import concourse.bass as bass
import concourse.mybir as mybir
from concourse.bass_utils import run_bass_kernel_spmd

F32 = mybir.dt.float32
BF16 = mybir.dt.bfloat16
I32 = mybir.dt.int32
F16 = mybir.dt.float16
AF = mybir.ActivationFunctionType
ALU = mybir.AluOpType
AX = mybir.AxisListType

S = 2048
OWN = 1024
D = 2048
NCORES = 8
DEBUG = False
DEBUG_NOGATE = False
STOP_AFTER = None


class _StopBuild(Exception):
    pass

TWO_PI = 2.0 * math.pi
HALF_PI = math.pi / 2.0
LN1E4_32 = math.log(10000.0) / 32.0


def _headpair(c2):
    m, j = divmod(c2, 4)
    return 8 * m + j, 8 * m + 4 + j


class Sched:
    SEMS = ("T", "V", "Sc", "G", "SY", "XD", "XO", "YD", "GC", "GL", "PQ", "KRD", "WQ", "WQ2", "QF", "QF2", "SLAB", "SLAB2", "DO")

    def __init__(self):
        self.steps = {e: [] for e in ("tensor", "vector", "scalar", "gpsimd", "sync")}
        self.cnt = {s: 0 for s in self.SEMS}
        self.labels = {}

    DMA_SEMS = ("SY", "XD", "XO", "YD", "GC", "GL", "PQ", "KRD", "WQ", "WQ2", "QF", "QF2", "SLAB", "SLAB2", "DO")

    def add(self, eng, fn, waits=(), label=None, sem=None):
        if sem is None:
            sem = {"tensor": "T", "vector": "V", "scalar": "Sc", "gpsimd": "G"}[eng]
        inc = 16 if sem in self.DMA_SEMS else 1
        do_inc = label is not None
        if do_inc:
            self.cnt[sem] += inc
            self.labels[label] = (sem, self.cnt[sem])
        for w in waits:
            assert isinstance(w, tuple) or w in self.labels, f"unknown label {w!r}"
        self.steps[eng].append((fn, tuple(waits), sem if do_inc else None, inc))

    def raw(self, eng, fn):
        self.steps[eng].append((fn, (), None, 0))

    def emit(self, eng_name, eng, sems):
        hw = {}
        for fn, waits, sem, inc in self.steps[eng_name]:
            for w in waits:
                wsem, wval = w if isinstance(w, tuple) else self.labels[w]
                if hw.get(wsem, -1) < wval:
                    eng.wait_ge(sems[wsem], wval)
                    hw[wsem] = wval
            ins = fn(eng)
            if sem is not None:
                assert ins is not None
                ins.then_inc(sems[sem], inc)


def _perm_order():
    order = []
    for c2 in range(16):
        ht, hb = _headpair(c2)
        order += list(range(64 * ht, 64 * ht + 64))
        order += list(range(64 * hb, 64 * hb + 64))
    return np.asarray(order)


def _consts():
    c = {}
    c["I128"] = np.eye(128, dtype=np.float32)
    c["I8"] = np.eye(8, dtype=np.float32)
    kk, qq = np.meshgrid(np.arange(128), np.arange(128), indexing="ij")
    c["maskD"] = (kk <= qq).astype(np.float32)
    ma = np.zeros((4, 128, 512), np.float32)
    for i in range(4):
        kp, qp = np.meshgrid(np.arange(128), np.arange(512), indexing="ij")
        ma[i] = np.where(qp < 128 * i + kp, -1e9, 0.0)
    c["maskAdd"] = np.ascontiguousarray(ma.transpose(1, 0, 2).reshape(128, 4 * 512))
    c["permQ"] = np.zeros((128, 128), np.float32)
    c["permK"] = np.zeros((128, 128), np.float32)
    for p2 in range(128):
        c["permQ"][p2 ^ 32, p2] = 1.0
        c["permK"][p2 ^ 16, p2] = 1.0
    for par in range(2):
        selR = np.zeros((128, 128), np.float32)
        selN = np.zeros((128, 128), np.float32)
        for half in range(2):
            lh = 2 * par + half
            for d in range(64):
                pp = 64 * half + d
                if d < 16:
                    selR[32 * lh + d, pp] = 1.0
                elif d < 32:
                    selN[32 * lh + (d - 16), pp] = 1.0
                elif d < 48:
                    selR[32 * lh + 16 + (d - 32), pp] = 1.0
                else:
                    selN[32 * lh + 16 + (d - 48), pp] = 1.0
        c[f"selR{par}"] = selR
        c[f"selN{par}"] = selN
    rexp = np.zeros((8, 128), np.float32)
    for e in range(8):
        rexp[e, 16 * e:16 * e + 16] = 16.0
    c["Rexp"] = rexp
    c["onesc"] = np.ones((128, 1), np.float32)
    e2 = np.zeros((65, 64), np.float32)
    e2[0] = np.eye(8, dtype=np.float32).reshape(64)
    e2[64] = e2[0]
    c["E2"] = e2
    sels = np.zeros((8, 4, 128), np.float32)
    for s_ in range(4):
        for pp in range(128):
            sels[4 * (pp // 64) + s_, s_, pp] = 1.0
    c["SelS"] = np.ascontiguousarray(sels.transpose(0, 1, 2).reshape(8, 512))
    p = np.arange(128)
    c["signQ"] = np.where((p % 64) < 32, -1.0, 1.0).astype(np.float32).reshape(128, 1)
    c["signK"] = np.where((p % 32) < 16, -1.0, 1.0).astype(np.float32).reshape(128, 1)
    c["imodQ"] = (p % 32).astype(np.float32).reshape(128, 1)
    c["imodK"] = (p % 16).astype(np.float32).reshape(128, 1)
    return c


def _tile_w(wT, gsz):
    n = wT.shape[1] // gsz
    t = wT.reshape(16, 128, n, gsz).transpose(2, 1, 0, 3)
    return np.ascontiguousarray(t.reshape(n, 128, 16 * gsz))


def _prep_inputs(inp):
    x = np.asarray(inp["L0_attn_hidden_states"], np.float32)
    qperm = _perm_order()
    qwT = np.ascontiguousarray(np.asarray(inp["L0_attn_q_w"], np.float32)[qperm, :].T)
    owT = np.ascontiguousarray(np.asarray(inp["L0_attn_o_w"], np.float32).T[qperm, :])
    fwT = np.ascontiguousarray(np.asarray(inp["L0_ffn_weight"], np.float32).T)
    krT = np.asarray(inp["L0_attn_kr_w"], np.float32).T
    dwT = np.asarray(inp["L0_attn_down_w"], np.float32).T
    upkT = np.asarray(inp["L0_attn_upk_w"], np.float32).T
    upvT = np.asarray(inp["L0_attn_upv_w"], np.float32).T
    gwT = np.asarray(inp["L0_ffn_gate_w"], np.float32).T
    AT = np.ascontiguousarray(np.asarray(inp["L0_ffn_A_w"], np.float32).reshape(128, 2048).T)
    BT = np.ascontiguousarray(np.asarray(inp["L0_ffn_B_w"], np.float32).transpose(0, 2, 1).reshape(128, 2048))

    shared = dict(
        qwTt=_tile_w(qwT, 128),
        fwTt=_tile_w(fwT, 512),
        owTt=np.ascontiguousarray(
            owT.reshape(16, 128, 4, 512).transpose(2, 1, 0, 3).reshape(4, 128, 8192)),
        krTt=np.ascontiguousarray(krT.reshape(16, 128, 256).transpose(1, 0, 2).reshape(128, 4096)),
        dwTt=np.ascontiguousarray(dwT.reshape(16, 128, 256).transpose(1, 0, 2).reshape(128, 4096)),
        upkTt=np.ascontiguousarray(upkT.reshape(2, 128, 256).transpose(1, 0, 2).reshape(128, 512)),
        upvTt=np.ascontiguousarray(upvT.reshape(2, 128, 512).transpose(1, 0, 2).reshape(128, 1024)),
        gwTt=np.ascontiguousarray(gwT.reshape(16, 128, 8).transpose(1, 0, 2).reshape(128, 128)),
        ATt=np.ascontiguousarray(AT.reshape(16, 128, 128).transpose(1, 0, 2).reshape(128, 2048)),
        BTt=BT,
    )
    shared.update(_consts())

    in_maps = []
    for c in range(NCORES):
        b, h = divmod(c, 2)
        blocks = ([1, 2, 0, 3] if h == 0 else [0, 3, 1, 2])
        x_in = np.ascontiguousarray(
            np.concatenate([x[b, 512 * p:512 * (p + 1)] for p in blocks], axis=0))
        bias_kill = np.full((128, 32), -10.0, np.float32)
        if h == 0:
            bias_kill[:, 0:4] = -1e30
        else:
            bias_kill[:, 16 + 4:16 + 8] = -1e30
        poff = np.zeros((128, 8), np.float32)
        poff[:, 0:4] = np.array([512.0 * p for p in blocks], np.float32)[None, :]
        poff[:, 4:6] = np.array([512.0 * p for p in blocks[2:]], np.float32)[None, :]
        d = dict(shared)
        d["x_in"] = x_in
        d["biasKill"] = bias_kill
        d["poff"] = poff
        in_maps.append(d)
    return in_maps


def build_nc(debug=False):
    nc = bass.Bass("TRN2", target_bir_lowering=False)

    def din(name, shape):
        return nc.dram_tensor(name, list(shape), F32, kind="ExternalInput").ap()

    x_in = din("x_in", (S, D))
    qwTt = din("qwTt", (16, 128, 2048))
    fwTt = din("fwTt", (4, 128, 8192))
    owTt = din("owTt", (4, 128, 8192))
    krTt = din("krTt", (128, 4096))
    dwTt = din("dwTt", (128, 4096))
    upkTt = din("upkTt", (128, 512))
    upvTt = din("upvTt", (128, 1024))
    gwTt = din("gwTt", (128, 128))
    ATt_d = din("ATt", (128, 2048))
    BTt_d = din("BTt", (128, 2048))
    I128_d = din("I128", (128, 128))
    I8_d = din("I8", (8, 8))
    maskD_d = din("maskD", (128, 128))
    maskAdd_d = din("maskAdd", (128, 2048))
    permQ_d = din("permQ", (128, 128))
    permK_d = din("permK", (128, 128))
    selR_d = [din(f"selR{p}", (128, 128)) for p in range(2)]
    selN_d = [din(f"selN{p}", (128, 128)) for p in range(2)]
    E2_d = din("E2", (65, 64))
    SelS_d = din("SelS", (8, 512))
    Rexp_d = din("Rexp", (8, 128))
    onesc_d = din("onesc", (128, 1))
    signQ_d = din("signQ", (128, 1))
    signK_d = din("signK", (128, 1))
    imodQ_d = din("imodQ", (128, 1))
    imodK_d = din("imodK", (128, 1))
    biasKill_d = din("biasKill", (128, 32))
    poff_d = din("poff", (128, 8))

    y_out = nc.dram_tensor("y", [OWN, D], F32, kind="ExternalOutput").ap()
    dbg = {}
    if debug:
        for nm, shp in [("d_xn", (128, 32768)), ("d_k", (128, 4 * S)),
                        ("d_v", (128, 16 * 520)), ("d_q", (128, 16 * OWN)),
                        ("d_at", (128, 16 * OWN)), ("d_cw", (128, 64)),
                        ("d_ts", (128, OWN))]:
            dbg[nm] = nc.dram_tensor(nm, list(shp), F32, kind="ExternalOutput").ap()

    K = Sched()
    cx = contextlib.ExitStack()
    T = lambda nm, shp, dt: cx.enter_context(nc.sbuf_tensor(nm, list(shp), dt))

    xnO = T("xnO", (128, 16 * OWN), BF16)
    attnO = T("attnO", (128, 16 * OWN), BF16)
    cosQ = T("cosQ", (128, OWN), BF16)
    sinQ = T("sinQ", (128, OWN), BF16)
    repS = T("repS", (128, 512), BF16)
    tsS = T("tsS", (128, OWN), BF16)
    I128 = T("I128s", (128, 128), BF16)
    onesb = T("onesb", (128, 1), BF16)
    biasK = T("biasKs", (128, 32), F32)
    csc = T("csc", (128, 4), F32)
    stat6 = T("stat6", (128, 48), F32)
    aggr2 = T("aggr2", (128, 8), F32)
    nmrS = T("nmrS", (128, 2), F32)
    invQ = T("invQ", (128, 1), F32)
    invK = T("invK", (128, 1), F32)
    signQs = T("signQs", (128, 1), F32)
    signKs = T("signKs", (128, 1), F32)
    imodQs = T("imodQs", (128, 1), F32)
    imodKs = T("imodKs", (128, 1), F32)
    poffS = T("poffS", (128, 8), F32)
    CC_M5, CC_EPS, CC_125 = 0, 1, 2

    pA = cx.enter_context(nc.psum_tensor("pA", [128, 2048], F32))
    pB = cx.enter_context(nc.psum_tensor("pB", [128, 1024], F32))
    pC = cx.enter_context(nc.psum_tensor("pC", [128, 512], F32))
    pD = cx.enter_context(nc.psum_tensor("pD", [128, 512], F32))

    sems = {s_: cx.enter_context(nc.semaphore(s_)) for s_ in Sched.SEMS}

    open_stacks = [cx]
    kv_cx = contextlib.ExitStack()
    open_stacks.append(kv_cx)
    Tk = lambda nm, shp, dt: kv_cx.enter_context(nc.sbuf_tensor(nm, list(shp), dt))
    kT = Tk("kT", (128, 4 * S), BF16)
    vT = Tk("vT", (128, 16 * 520), BF16)

    s1_cx = contextlib.ExitStack()
    open_stacks.append(s1_cx)
    T1 = lambda nm, shp, dt: s1_cx.enter_context(nc.sbuf_tensor(nm, list(shp), dt))
    xnP = T1("xnP", (128, 16 * 1024), BF16)
    xbuf = [T1(f"xbuf{i}", (128, D), F32) for i in range(2)]
    xnt = T1("xnt", (128, D), BF16)
    iotaB = T1("iotaB", (128, S), F16)
    scr0 = T1("scr0", (128, 512), F32)
    scr1 = T1("scr1", (128, 512), F32)
    cosK = T1("cosK", (128, S), BF16)
    sinK = T1("sinK", (128, S), BF16)
    krA = T1("krA", (128, 4096), BF16)
    dwS = T1("dwS", (128, 4096), BF16)
    upk = T1("upk", (128, 512), BF16)
    upv = T1("upv", (128, 1024), BF16)
    ckvS = T1("ckvS", (128, 2 * S), BF16)
    rkr = T1("rkr", (128, 512), BF16)
    kcs = T1("kcs", (128, 512), BF16)
    rop0 = T1("rop0", (128, 512), BF16)
    rop1 = T1("rop1", (128, 512), BF16)
    permKs = T1("permKs", (128, 128), BF16)
    selRs = [T1(f"selRs{p}", (128, 128), BF16) for p in range(2)]
    selNs = [T1(f"selNs{p}", (128, 128), BF16) for p in range(2)]

    def dma(eng_name, dst, src, label=None, waits=(), sem="SY"):
        K.add(eng_name, lambda e, d=dst, s=src: e.dma_start(out=d, in_=s),
              waits=waits, label=label, sem=sem)

    dma("sync", biasK[:], biasKill_d[:, :], label="c_bk")
    dma("sync", signQs[:], signQ_d[:, :], label="c_sq")
    dma("sync", signKs[:], signK_d[:, :], label="c_sk")
    dma("sync", imodQs[:], imodQ_d[:, :], label="c_iq")
    dma("sync", imodKs[:], imodK_d[:, :], label="c_ik")
    dma("sync", poffS[:], poff_d[:, :], label="c_poff")
    dma("gpsimd", I128[:], I128_d[:, :], label="c_i128", sem="GC")
    dma("gpsimd", onesb[:], onesc_d[:, :], label="c_ones", sem="GC")
    dma("gpsimd", permKs[:], permK_d[:, :], label="c_pk", sem="GC")
    for p in range(2):
        dma("gpsimd", selRs[p][:], selR_d[p][:, :], label=f"c_selr{p}", sem="GC")
        dma("gpsimd", selNs[p][:], selN_d[p][:, :], label=f"c_seln{p}", sem="GC")
    dma("gpsimd", upk[:], upkTt[:, :], label="c_upk", sem="GC")
    dma("gpsimd", upv[:], upvTt[:, :], label="c_upv", sem="GC")
    dma("gpsimd", krA[:], krTt[:, :], label="c_kra", sem="GC")
    dma("gpsimd", dwS[:], dwTt[:, :], label="c_dws", sem="GC")
    for _lbl in ("c_bk", "c_sq", "c_sk", "c_iq", "c_ik", "c_poff"):
        K.labels[_lbl] = ("SY", 6 * 16)
    for _lbl in ("c_i128", "c_ones", "c_pk", "c_selr0",
                 "c_seln0", "c_selr1", "c_seln1", "c_upk", "c_upv", "c_kra",
                 "c_dws"):
        K.labels[_lbl] = ("GC", K.cnt["GC"])

    def f_csc(v):
        v.memset(csc[:, CC_M5:CC_M5 + 1], -0.5)
        v.memset(csc[:, CC_EPS:CC_EPS + 1], 1e-5)
        v.memset(csc[:, CC_125:CC_125 + 1], 0.125)
        return v.memset(
            vT[:].rearrange("p (b c) -> p b c", c=65)[:, :, 64:65], 1.0)
    K.add("vector", f_csc, label="v_csc")
    K.add("gpsimd", lambda g: g.iota(iotaB[:], [[1, S]], base=0, channel_multiplier=0,
                                     allow_small_or_imprecise_dtypes=True),
          label="g_iota")
    _stop_label = {"p1": "v_ev15_3", "p2": "v_vev15",
                   "p3": "v_qr3_15", "p4": "v_nm15_1", "p5pre": "v_ts",
                   "g1": "v_lgev", "g2": "v_m1", "g3": "s_pe", "g4": "v_cw",
                   "g5": "v_cwtev7", "g6": "v_cwrev",
                   "g3a": "v_eq1", "g3b": "v_p2", "g3c": "v_m2",
                   "g3d": "v_kp2", "g3e": "v_rcd"}
    _marks = {}
    def _chk(tag):
        _marks[tag] = {e: len(K.steps[e]) for e in K.steps}

    def table(pref, imod_t, sign_t, inv_t, cosT, sinT, ncols, prev_last):
        units = []
        unit = [("vector",
                 lambda v: v.tensor_single_scalar(inv_t[:], imod_t[:], -LN1E4_32, ALU.mult),
                 ["c_iq" if pref == "Q" else "c_ik", "c_poff",
                  "c_sq" if pref == "Q" else "c_sk"],
                 f"v_{pref}inv0"),
                ("scalar",
                 lambda s: s.activation(inv_t[:], inv_t[:], AF.Exp),
                 [f"v_{pref}inv0"], f"s_{pref}inv")]
        half = 512
        last = prev_last
        for hh in range(ncols // 512):
            for trig in range(2):
                tag = f"{pref}{hh}{trig}"
                pcol = (4 + hh) if pref == "Q" else hh
                chain = [
                    lambda v, half=half, pcol=pcol: v.tensor_scalar(
                        scr0[:, 0:half], iotaB[:, 0:half], poffS[:, pcol:pcol + 1],
                        inv_t[:, 0:1], ALU.add, ALU.mult),
                ]
                if trig == 1:
                    chain.append(lambda v, half=half: v.tensor_single_scalar(
                        scr0[:, 0:half], scr0[:, 0:half], HALF_PI, ALU.add))
                chain += [
                    lambda v, half=half: v.tensor_single_scalar(
                        scr1[:, 0:half], scr0[:, 0:half], 1.0 / TWO_PI, ALU.mult),
                    lambda v, half=half: v.tensor_copy(
                        scr1[:, 0:half].bitcast(I32), scr1[:, 0:half]),
                    lambda v, half=half: v.tensor_copy(
                        scr1[:, 0:half], scr1[:, 0:half].bitcast(I32)),
                    lambda v, half=half: v.scalar_tensor_tensor(
                        scr0[:, 0:half], scr1[:, 0:half], -TWO_PI, scr0[:, 0:half],
                        ALU.mult, ALU.add),
                    lambda v, half=half: v.tensor_single_scalar(
                        scr1[:, 0:half], scr0[:, 0:half], math.pi, ALU.is_gt),
                    lambda v, half=half: v.scalar_tensor_tensor(
                        scr0[:, 0:half], scr1[:, 0:half], -TWO_PI, scr0[:, 0:half],
                        ALU.mult, ALU.add),
                    lambda v, half=half: v.tensor_single_scalar(
                        scr0[:, 0:half], scr0[:, 0:half], math.pi, ALU.min),
                    lambda v, half=half: v.tensor_single_scalar(
                        scr0[:, 0:half], scr0[:, 0:half], -math.pi, ALU.max),
                ]
                prev = None
                for i, st in enumerate(chain):
                    w = [prev] if prev else ([f"s_{pref}inv", "g_iota"] +
                                             ([last] if last else []))
                    unit.append(("vector", st, w, f"v_rr{tag}_{i}"))
                    prev = f"v_rr{tag}_{i}"
                unit.append(("scalar", lambda s, half=half: s.activation(
                    scr1[:, 0:half], scr0[:, 0:half], AF.Sin),
                    [prev], f"s_sin{tag}"))
                cols = slice(hh * half, (hh + 1) * half)
                if trig == 0:
                    unit.append(("vector", lambda v, cols=cols, half=half, dstT=sinT: v.tensor_scalar(
                        dstT[:, cols], scr1[:, 0:half], sign_t[:, 0:1], None, ALU.mult),
                        [f"s_sin{tag}"], f"v_tab{tag}"))
                else:
                    unit.append(("vector", lambda v, cols=cols, half=half, dstT=cosT: v.tensor_copy(
                        dstT[:, cols], scr1[:, 0:half]),
                        [f"s_sin{tag}"], f"v_tab{tag}"))
                last = f"v_tab{tag}"
                units.append(unit)
                unit = []
        return units, last

    xsrc = x_in.rearrange("(n p) d -> n p d", p=128)
    for sc in range(16):
        bsel = xbuf[sc % 2]
        w = [f"v_xnt{sc-2}"] if sc >= 2 else []
        dma("sync", bsel[:], xsrc[sc], label=f"ds_x{sc}", waits=w, sem="XD")

        pp = sc % 2
        a0 = 4 * pp

        def f_st(v, b=bsel, pp=pp):
            ins = None
            for cchunk in range(4):
                ins = v.bn_stats(stat6[:, 24 * pp + cchunk * 6:24 * pp + cchunk * 6 + 6],
                                 b[:, cchunk * 512:(cchunk + 1) * 512])
            return ins
        K.add("vector", f_st,
              waits=[("XD", 16 * min(sc + 2, 16)), "v_csc"] +
                    ([f"v_ag{sc-2}"] if sc >= 2 else []),
              label=f"v_st{sc}")
        K.add("vector", lambda v, pp=pp, a0=a0: v.bn_aggr(
            aggr2[:, a0:a0 + 2], stat6[:, 24 * pp:24 * pp + 24]),
              waits=[f"v_st{sc}"] + ([f"s_ln{sc-2}", f"v_nmr{sc-2}"] if sc >= 2 else []),
              label=f"v_ag{sc}")
        K.add("scalar", lambda s, a0=a0: s.activation(
            aggr2[:, a0 + 2:a0 + 3], aggr2[:, a0 + 1:a0 + 2], AF.Ln,
            bias=csc[:, CC_EPS:CC_EPS + 1]),
              waits=[f"v_ag{sc}"] + ([f"s_rstd{sc-2}"] if sc >= 2 else []),
              label=f"s_ln{sc}")
        K.add("scalar", lambda s, a0=a0: s.activation(
            aggr2[:, a0 + 3:a0 + 4], aggr2[:, a0 + 2:a0 + 3], AF.Exp,
            scale=csc[:, CC_M5:CC_M5 + 1]),
              waits=[f"s_ln{sc}"] + ([f"v_xnt{sc-2}"] if sc >= 2 else []),
              label=f"s_rstd{sc}")
        K.add("vector", lambda v, pp=pp, a0=a0: v.scalar_tensor_tensor(
            nmrS[:, pp:pp + 1], aggr2[:, a0:a0 + 1], -1.0, aggr2[:, a0 + 3:a0 + 4],
            ALU.mult, ALU.mult),
            waits=[f"s_rstd{sc}"], label=f"v_nmr{sc}")

        def f_xnt(sC, b=bsel, pp=pp, a0=a0):
            return sC.activation(xnt[:], b[:], AF.Identity,
                                 bias=nmrS[:, pp:pp + 1],
                                 scale=aggr2[:, a0 + 3:a0 + 4])
        K.add("scalar", f_xnt,
              waits=[f"v_nmr{sc}"] + ([f"v_ev{sc-1}_3"] if sc >= 1 else []),
              label=f"v_xnt{sc}")
        for g in range(4):
            pt = pC if g % 2 == 0 else pD

            def f_tr(t, g=g, pt=pt):
                ins = None
                for jj in range(4):
                    d2 = 4 * g + jj
                    ins = t.matmul(pt[:, jj * 128:(jj + 1) * 128],
                                   xnt[:, d2 * 128:(d2 + 1) * 128], I128[:],
                                   start=True, stop=True)
                return ins
            w = [f"v_xnt{sc}", "c_i128"]
            if sc > 0 or g >= 2:
                pg = (sc, g - 2) if g >= 2 else (sc - 1, g + 2)
                w.append("v_ev%d_%d" % pg)
            K.add("tensor", f_tr, waits=w, label=f"t_tr{sc}_{g}")

            def f_ev(v, sc=sc, g=g, pt=pt):
                dst = xnP if sc < 8 else xnO
                col = (sc % 8) * 128
                return v.tensor_copy(
                    dst[:].rearrange("p (d2 c) -> p d2 c", c=1024)[:, 4 * g:4 * g + 4, col:col + 128],
                    pt[:].rearrange("p (j c) -> p j c", c=128))
            K.add("vector", f_ev, waits=[f"t_tr{sc}_{g}"], label=f"v_ev{sc}_{g}")

    _chk("p1")

    tabK, tk_done = table("K", imodKs, signKs, invK, cosK, sinK, S, None)
    tabQ, tq_done = table("Q", imodQs, signQs, invQ, cosQ, sinQ, OWN, tk_done)

    def emit_units(us):
        for u in us:
            for eng, fn, w, lab in u:
                K.add(eng, fn, waits=w, label=lab)

    def xn_mov(d2, ng):
        if ng < 2:
            return xnP[:, d2 * 1024 + ng * 512: d2 * 1024 + (ng + 1) * 512]
        return xnO[:, d2 * 1024 + (ng - 2) * 512: d2 * 1024 + (ng - 1) * 512]

    def xn_ready(ng):
        return [f"v_ev{sc}_{g}" for sc in range(4 * ng, 4 * ng + 4) for g in range(4)]

    for ng in range(4):
        for d2 in range(16):
            lin = ng * 16 + d2

            def f_mm(t, ng=ng, d2=d2, lin=lin):
                st, sp = d2 == 0, d2 == 15
                ins = None
                for cc in range(2):
                    t.matmul(pA[:, cc * 512:(cc + 1) * 512],
                             krA[:, d2 * 256 + cc * 128: d2 * 256 + (cc + 1) * 128],
                             xn_mov(d2, ng), start=st, stop=sp)
                for cc in range(2):
                    ins = t.matmul(pA[:, (2 + cc) * 512:(3 + cc) * 512],
                                   dwS[:, d2 * 256 + cc * 128: d2 * 256 + (cc + 1) * 128],
                                   xn_mov(d2, ng), start=st, stop=sp)
                return ins
            w = ["c_kra", "c_dws"]
            if d2 == 0:
                w += xn_ready(ng)
                if ng > 0:
                    w += [f"v_krev{ng-1}_1", f"v_ckv{ng-1}"]
            K.add("tensor", f_mm, waits=w, label=f"t_kr{ng}_{d2}")

        if ng < 2:
            emit_units(tabK[4 * ng:4 * ng + 4])
        else:
            emit_units(tabQ[2 * (ng - 2):2 * (ng - 2) + 2])
        K.add("vector", lambda v, ng=ng: v.tensor_copy(
            ckvS[:].rearrange("p (cc s) -> p cc s", cc=2)[:, :, ng * 512:(ng + 1) * 512],
            pA[:, 1024:2048].rearrange("p (cc s) -> p cc s", cc=2)),
            waits=[f"t_kr{ng}_15"], label=f"v_ckv{ng}")

        for cc in range(2):
            w = [f"t_kr{ng}_15"]
            if (ng, cc) != (0, 0):
                pv = (ng, 0) if cc == 1 else (ng - 1, 1)
                w.append("t_as%d_%d_1" % pv)
            K.add("vector", lambda v, cc=cc: v.tensor_copy(
                rkr[:, 0:512], pA[:, cc * 512:(cc + 1) * 512]),
                waits=w, label=f"v_krev{ng}_{cc}")
            w = [f"v_krev{ng}_{cc}", "c_pk"]
            if (ng, cc) != (0, 0):
                pv = (ng, 0) if cc == 1 else (ng - 1, 1)
                w.append("v_krrope%d_%d" % pv)
            K.add("tensor", lambda t, cc=cc: t.matmul(
                pB[:, 0:512], permKs[:], rkr[:, 0:512], start=True, stop=True),
                waits=w, label=f"t_krsw{ng}_{cc}")
            K.add("vector", lambda v, ng=ng, cc=cc: v.tensor_tensor(
                rop0[:], pB[:, 0:512], sinK[:, ng * 512:(ng + 1) * 512], ALU.mult),
                waits=[f"t_krsw{ng}_{cc}", f"v_tabK{ng}1"], label=f"v_krm1{ng}_{cc}")
            K.add("vector", lambda v, ng=ng, cc=cc: v.tensor_tensor(
                rop1[:], rkr[:, 0:512], cosK[:, ng * 512:(ng + 1) * 512], ALU.mult),
                waits=[f"v_krm1{ng}_{cc}"], label=f"v_krm2{ng}_{cc}")
            K.add("vector", lambda v: v.tensor_tensor(
                rkr[:, 0:512], rop0[:], rop1[:], ALU.add),
                waits=[f"v_krm2{ng}_{cc}"], label=f"v_krrope{ng}_{cc}")
            def f_kc(t, ng=ng, cc=cc):
                ins = None
                for csrc in range(2):
                    ins = t.matmul(pB[:, 512:1024],
                                   upk[:, csrc * 256 + cc * 128: csrc * 256 + (cc + 1) * 128],
                                   ckvS[:, csrc * 2048 + ng * 512: csrc * 2048 + (ng + 1) * 512],
                                   start=(csrc == 0), stop=(csrc == 1))
                return ins
            w = [f"v_ckv{ng}", "c_upk"]
            if (ng, cc) != (0, 0):
                pv = (ng, 0) if cc == 1 else (ng - 1, 1)
                w.append("v_kcev%d_%d" % pv)
            K.add("tensor", f_kc, waits=w, label=f"t_kc{ng}_{cc}")
            w = [f"t_kc{ng}_{cc}"]
            if (ng, cc) != (0, 0):
                pv = (ng, 0) if cc == 1 else (ng - 1, 1)
                w.append("t_as%d_%d_1" % pv)
            K.add("vector", lambda v: v.tensor_copy(kcs[:, 0:512], pB[:, 512:1024]),
                  waits=w, label=f"v_kcev{ng}_{cc}")
            for par in range(2):
                m = 2 * cc + par

                def f_as(t, par=par):
                    t.matmul(pC[:, 0:512], selRs[par][:], rkr[:, 0:512],
                             start=True, stop=False)
                    return t.matmul(pC[:, 0:512], selNs[par][:], kcs[:, 0:512],
                                    start=False, stop=True)
                w = [f"v_krrope{ng}_{cc}", f"v_kcev{ng}_{cc}",
                     f"c_selr{par}", f"c_seln{par}"]
                if (ng, cc, par) == (0, 0, 0):
                    w.append("v_ev15_2")
                else:
                    pv = (ng, cc, 0) if par == 1 else \
                        ((ng, cc - 1, 1) if cc == 1 else (ng - 1, 1, 1))
                    w.append("v_kas%d_%d_%d" % pv)
                K.add("tensor", f_as, waits=w, label=f"t_as{ng}_{cc}_{par}")
                K.add("vector", lambda v, ng=ng, m=m: v.tensor_copy(
                    kT[:, m * 2048 + ng * 512: m * 2048 + (ng + 1) * 512], pC[:, 0:512]),
                    waits=[f"t_as{ng}_{cc}_{par}"], label=f"v_kas{ng}_{cc}_{par}")

    for sc in range(16):
        vps = pD[:, 0:512] if sc % 2 == 0 else pA[:, 0:512]

        def f_v(t, sc=sc, vps=vps):
            ins = None
            for csrc in range(2):
                ins = t.matmul(vps,
                               ckvS[:, csrc * 2048 + sc * 128: csrc * 2048 + (sc + 1) * 128],
                               upv[:, csrc * 512:(csrc + 1) * 512],
                               start=(csrc == 0), stop=(csrc == 1))
            return ins
        w = [f"v_ckv{sc // 4}", "c_upv"]
        if sc >= 2:
            w.append(f"v_vev{sc-2}")
        elif sc == 0:
            w.append("v_ev15_3")
        else:
            w.append("v_krev3_1")
        K.add("tensor", f_v, waits=w, label=f"t_v{sc}")
        K.add("vector", lambda v, sc=sc, vps=vps: v.tensor_copy(
            vT[:, sc * 520:(sc + 1) * 520]
              .rearrange("p (m h c) -> p m h c", h=2, c=65)[:, :, :, 0:64],
            vps.rearrange("p (m h c) -> p m h c", h=2, c=64)),
            waits=[f"t_v{sc}"], label=f"v_vev{sc}")

    if debug:
        K.add("gpsimd", lambda g: g.dma_start(out=dbg["d_k"][:, :], in_=kT[:]),
              waits=["v_kas3_1_1"], label="dbg_k", sem="DO")
        K.add("gpsimd", lambda g: g.dma_start(out=dbg["d_v"][:, :], in_=vT[:]),
              waits=["v_vev15"], label="dbg_v", sem="DO")
        K.add("gpsimd", lambda g: g.dma_start(out=dbg["d_xn"][:, 0:16384], in_=xnP[:]),
              waits=["v_ev15_3", "v_ev15_2"], label="dbg_xn", sem="DO")
        K.add("gpsimd", lambda g: g.dma_start(out=dbg["d_xn"][:, 16384:32768], in_=xnO[:]),
              waits=["v_ev15_3", "v_ev15_2"], label="dbg_xn2", sem="DO")

    _chk("p2")
    s1_cx.close()
    open_stacks.remove(s1_cx)
    S1_LAST = ["v_kas3_1_1", "v_vev15"]
    if debug and not DEBUG_NOGATE:
        S1_LAST.append(("DO", K.cnt["DO"]))
    s2_cx = contextlib.ExitStack()
    open_stacks.append(s2_cx)
    T2 = lambda nm, shp, dt: s2_cx.enter_context(nc.sbuf_tensor(nm, list(shp), dt))
    qT = T2("qT", (128, 16 * OWN), BF16)
    pbuf = [T2(f"pbuf{i}", (128, 1024), BF16) for i in range(4)]
    wq = [T2(f"wq{i}", (128, 2048), BF16) for i in range(3)]
    qsS = T2("qsS", (128, OWN), BF16)
    qswt1 = T2("qswt1", (128, OWN), BF16)
    I128f = T2("I128f", (128, 128), F32)
    dma("sync", I128f[:], I128_d[:, :], label="c_i128f", waits=S1_LAST)
    E2 = T2("E2s", (65, 64), F32)
    SelS = T2("SelSs", (8, 512), F32)
    dnS = T2("dnS", (65, 512), F32)
    maskA = T2("maskAs", (128, 2048), BF16)
    dma("gpsimd", maskA[:], maskAdd_d[:, :], label="c_maska", waits=S1_LAST,
        sem="KRD")
    I8s = T2("I8s", (8, 8), F32)
    RexpS = T2("RexpS", (8, 128), F32)
    dma("sync", I8s[:], I8_d[:, :], label="c_i8", waits=S1_LAST)
    dma("sync", RexpS[:], Rexp_d[:, :], label="c_rexp", waits=S1_LAST)
    rTS = T2("rTS", (128, 8), F32)
    rT8 = T2("rT8", (8, 128), F32)
    dma("sync", E2[:], E2_d[:, :], label="c_e164", waits=S1_LAST)
    dma("sync", SelS[:], SelS_d[:, :], label="c_sels", waits=S1_LAST)
    permQs = T2("permQs", (128, 128), BF16)
    dma("gpsimd", permQs[:], permQ_d[:, :], label="c_pq", waits=S1_LAST, sem="PQ")
    qf32 = [T2(f"qf32_{i}", (128, 2048), F32) for i in range(2)]

    qw_ready = {}
    for qc in range(16):
        w = [f"t_q{qc-3}_mm"] if qc >= 3 else list(S1_LAST)
        k = qc // 2
        if qc % 2 == 1:
            sm = "WQ" if k % 2 == 0 else "WQ2"
            dma("gpsimd", wq[qc % 3][:], qwTt[qc], label=f"dg_wq{qc}", sem=sm,
                waits=w)
            qw_ready[qc] = (sm, 16 * (k // 2 + 1))
        else:
            wf = list(w)
            if qc >= 4:
                wf.append(f"s_qc{qc-4}")
            sm = "QF" if k % 2 == 0 else "QF2"
            dma("sync", qf32[k % 2][:], qwTt[qc], label=f"ds_wqf{qc}",
                sem=sm, waits=wf)
            K.add("scalar", lambda sC, qc=qc, k=k: sC.activation(
                wq[qc % 3][:], qf32[k % 2][:], AF.Copy),
                waits=[(sm, 16 * (k // 2 + 1))] + w, label=f"s_qc{qc}")
            qw_ready[qc] = f"s_qc{qc}"

        qps = pB[:, 0:1024] if qc % 2 == 0 else pA[:, 1024:2048]

        def f_qmm(t, qc=qc, qps=qps):
            ins = None
            for d2 in range(16):
                for ngo in range(2):
                    ins = t.matmul(qps[:, ngo * 512:(ngo + 1) * 512],
                                   wq[qc % 3][:, d2 * 128:(d2 + 1) * 128],
                                   xnO[:, d2 * 1024 + ngo * 512: d2 * 1024 + (ngo + 1) * 512],
                                   start=(d2 == 0), stop=(d2 == 15))
            return ins
        w = [qw_ready[qc]]
        if qc == 0:
            w += S1_LAST
        elif qc == 1:
            w.append("v_ckv3")
        else:
            w.append(f"v_qev{qc-2}")
        K.add("tensor", f_qmm, waits=w, label=f"t_q{qc}_mm")
        w = [f"t_q{qc}_mm"]
        if qc >= 1:
            w.append(f"v_qr2_{qc-1}")
        K.add("vector", lambda v, qps=qps: v.tensor_copy(qsS[:], qps),
              waits=w, label=f"v_qev{qc}")

        def f_qsw(t):
            t.matmul(pA[:, 0:512], permQs[:], qsS[:, 0:512], start=True, stop=True)
            return t.matmul(pA[:, 512:1024], permQs[:], qsS[:, 512:1024],
                            start=True, stop=True)
        w = [f"v_qev{qc}", "c_pq"]
        if qc == 0:
            w += ["v_ckv3", "v_krev3_1", "v_vev15"]
        else:
            w.append(f"v_qr1_{qc-1}")
        K.add("tensor", f_qsw, waits=w, label=f"t_qsw{qc}")
        K.add("vector", lambda v, qc=qc: v.tensor_tensor(
            qT[:, qc * 1024:(qc + 1) * 1024], pA[:, 0:1024], sinQ[:], ALU.mult),
            waits=[f"t_qsw{qc}", tq_done] + ([f"v_qr3_{qc-1}"] if qc >= 1 else []),
            label=f"v_qr1_{qc}")
        K.add("vector", lambda v: v.tensor_tensor(qswt1[:], qsS[:], cosQ[:], ALU.mult),
              waits=[f"v_qr1_{qc}"], label=f"v_qr2_{qc}")
        K.add("vector", lambda v, qc=qc: v.tensor_tensor(
            qT[:, qc * 1024:(qc + 1) * 1024], qT[:, qc * 1024:(qc + 1) * 1024],
            qswt1[:], ALU.add),
            waits=[f"v_qr2_{qc}"], label=f"v_qr3_{qc}")

    if debug:
        K.add("gpsimd", lambda g: g.dma_start(out=dbg["d_q"][:, :], in_=qT[:]),
              waits=["v_qr3_15"], label="dbg_q", sem="DO")

    _chk("p3")
    for _lbl in ("c_e164", "c_sels", "c_i128f", "c_rexp"):
        K.labels[_lbl] = ("SY", K.cnt["SY"])
    kci = 0
    g_idx = 0
    last_exp = {0: None, 1: None}
    last_pv4 = {i: None for i in range(4)}
    last_repev = {0: None, 1: None}
    prev_norm = None
    prev_repev = None
    pending_stages = []

    def _XA(p, r0, r1, c0, c1):
        return pC[r0:r1, c0:c1] if p == 0 else pB[r0:r1, c0:c1]

    def _YA(p, r0, r1, c0, c1):
        return pD[r0:r1, c0:c1] if p == 0 else pB[r0:r1, 512 + c0:512 + c1]

    for hp in range(16):
        m = hp // 4
        for qb in (0, 1):
            q0 = qb * 512
            nd = 4 if qb == 0 else 12
            dg0 = 8 if qb == 0 else 12
            chunks = [(kc, False) for kc in range(nd)] + \
                     [(dg0 + i, True) for i in range(4)]
            nch = len(chunks)
            kci_base = kci
            gp = g_idx % 2

            def emit_sc(ci, kc, par, isdiag, hp=hp, m=m, q0=q0, kci_base=kci_base):
                def f_sc(t, kc=kc, par=par, m=m, q0=q0, hp=hp, isdiag=isdiag):
                    qtop = qT[0:64, hp * 1024 + q0: hp * 1024 + q0 + 512]
                    qbot = qT[64:128, hp * 1024 + q0: hp * 1024 + q0 + 512]
                    i = kc % 4
                    t.matmul(pA[:, par * 1024: par * 1024 + 512],
                             kT[0:64, m * 2048 + kc * 128: m * 2048 + (kc + 1) * 128],
                             qtop, start=True, stop=not isdiag)
                    ins = t.matmul(pA[:, par * 1024 + 512: par * 1024 + 1024],
                                   kT[64:128, m * 2048 + kc * 128: m * 2048 + (kc + 1) * 128],
                                   qbot, start=True, stop=not isdiag,
                                   tile_position=(64, 0))
                    if isdiag:
                        t.matmul(pA[:, par * 1024: par * 1024 + 512],
                                 I128[:], maskA[:, 512 * i:512 * i + 512],
                                 start=False, stop=True)
                        ins = t.matmul(pA[:, par * 1024 + 512: par * 1024 + 1024],
                                       I128[:], maskA[:, 512 * i:512 * i + 512],
                                       start=False, stop=True)
                    return ins
                w = []
                kci0 = kci_base + ci
                if kci0 == 0:
                    w += ["v_qr3_15", "v_qr1_15", "c_maska", "c_i128"]
                if kci0 == 1:
                    w += ["v_qev15"]
                if last_exp[par]:
                    w.append(last_exp[par])
                K.add("tensor", f_sc, waits=w, label=f"t_sc{kci0}")

            def emit_rest(ci, kc, isdiag, par, first, lastc, kci_base=kci_base,
                          m=m, qb=qb, gp=gp, g_idx=g_idx):
                kci0 = kci_base + ci
                slot = kci0 % 4

                def f_ex(sC, par=par, kc=kc, slot=slot, qb=qb):
                    return sC.activation(pbuf[slot][:], pA[:, par * 1024:(par + 1) * 1024],
                                         AF.Exp, scale=csc[:, CC_125:CC_125 + 1],
                                         bias=biasK[:, qb * 16 + kc: qb * 16 + kc + 1])
                w = [f"t_sc{kci0}", "c_bk"]
                if last_pv4[slot]:
                    w.append(last_pv4[slot])
                K.add("scalar", f_ex, waits=w, label=f"s_ex{kci0}")
                last_exp[par] = f"s_ex{kci0}"
                pv_src = f"s_ex{kci0}"
                def f_pv(t, kc=kc, slot=slot, first=first, lastc=lastc, m=m, gp=gp):
                    c0 = kc * 520 + 130 * m
                    t.matmul(_XA(gp, 0, 65, 0, 512), vT[:, c0:c0 + 65],
                             pbuf[slot][:, 0:512], start=first, stop=lastc,
                             tile_position=(0, 0))
                    t.matmul(_YA(gp, 64, 128, 0, 512), vT[:, c0 + 65:c0 + 129],
                             pbuf[slot][:, 512:1024], start=first, stop=lastc,
                             tile_position=(0, 64))
                    return t.matmul(_YA(gp, 0, 1, 0, 512), onesb[:],
                                    pbuf[slot][:, 512:1024],
                                    start=first, stop=lastc, tile_position=(0, 0))
                w = [pv_src, "c_ones"]
                if first:
                    if last_repev[gp]:
                        w.append(last_repev[gp])
                    if kci0 == 0:
                        w += ["v_vev15", "v_qev15"]
                    if g_idx == 1:
                        w.append("v_qev15")
                K.add("tensor", f_pv, waits=w, label=f"t_pv{kci0}")
                last_pv4[slot] = f"t_pv{kci0}"

            emit_sc(0, chunks[0][0], kci % 2, chunks[0][1])
            emit_sc(1, chunks[1][0], (kci + 1) % 2, chunks[1][1])
            stages = pending_stages
            pending_stages = []
            for ci, (kc, isdiag) in enumerate(chunks):
                par = (kci_base + ci) % 2
                emit_rest(ci, kc, isdiag, par, ci == 0, ci == nch - 1)
                if ci < len(stages):
                    stages[ci]()
                if ci + 2 < nch:
                    emit_sc(ci + 2, chunks[ci + 2][0], (kci_base + ci + 2) % 2,
                            chunks[ci + 2][1])
                kci += 1

            lab = f"{hp}_{qb}"
            c0a = hp * 1024 + qb * 512
            kl = kci - 1
            bp = gp

            def s_dnraw(lab=lab, c0a=c0a, kl=kl, bp=bp):
                def f_dnr(v):
                    v.tensor_copy(dnS[64:65, 0:512], _XA(bp, 64, 65, 0, 512))
                    v.tensor_copy(dnS[0:1, 0:512], _YA(bp, 0, 1, 0, 512))
                    v.tensor_copy(attnO[0:64, c0a:c0a + 512], _XA(bp, 0, 64, 0, 512))
                    return v.tensor_copy(attnO[64:128, c0a:c0a + 512],
                                         _YA(bp, 64, 128, 0, 512))
                K.add("vector", f_dnr, waits=[f"t_pv{kl}"], label=f"v_dnr{lab}")

            def s_r1(lab=lab, bp=bp):
                def f_r1(t):
                    ins = None
                    for j in range(8):
                        rr = 64 if j < 4 else 0
                        sidx = j % 4
                        src = dnS[rr:rr + 1, 128 * sidx: 128 * sidx + 128]
                        ins = t.matmul(_XA(bp, 0, 128, 504, 512), src,
                                       E2[rr:rr + 1, 8 * j:8 * j + 8],
                                       start=(j == 0), stop=(j == 7),
                                       tile_position=(rr, 0))
                    return ins
                K.add("tensor", f_r1, waits=[f"v_dnr{lab}", "c_e164"],
                      label=f"t_r1_{lab}")

            def s_rts(lab=lab, bp=bp):
                K.add("vector",
                      lambda v: v.reciprocal(rTS[:], _XA(bp, 0, 128, 504, 512)),
                      waits=[f"t_r1_{lab}"], label=f"v_rts{lab}")

            def s_rt8(lab=lab, bp=bp):
                K.add("tensor",
                      lambda t: t.matmul(_XA(bp, 0, 8, 376, 504), rTS[:], I128f[:],
                                         start=True, stop=True),
                      waits=[f"v_rts{lab}", "c_i128f"], label=f"t_rt8_{lab}")

            def s_rt8ev(lab=lab, bp=bp):
                K.add("vector",
                      lambda v: v.tensor_copy(rT8[:], _XA(bp, 0, 8, 376, 504)),
                      waits=[f"t_rt8_{lab}"], label=f"v_rt8_{lab}")

            def s_rep(lab=lab, bp=bp):
                def f_rep(t):
                    ins = None
                    for sidx in range(4):
                        ins = t.matmul(_XA(bp, 0, 128, 128 * sidx, 128 * sidx + 128),
                                       SelS[:, 128 * sidx:128 * sidx + 128],
                                       rT8[:], start=True, stop=True)
                    return ins
                K.add("tensor", f_rep, waits=[f"v_rt8_{lab}", "c_sels"],
                      label=f"t_rep{lab}")

            def s_repev(lab=lab, bp=bp):
                K.add("vector",
                      lambda v: v.tensor_copy(repS[:], _XA(bp, 0, 128, 0, 512)),
                      waits=[f"t_rep{lab}"], label=f"v_repev{lab}")

            def s_nm(lab=lab, c0a=c0a):
                def f_nm(v):
                    v.tensor_tensor(attnO[0:64, c0a:c0a + 512],
                                    attnO[0:64, c0a:c0a + 512],
                                    repS[0:64, :], ALU.mult)
                    return v.tensor_tensor(attnO[64:128, c0a:c0a + 512],
                                           attnO[64:128, c0a:c0a + 512],
                                           repS[64:128, :], ALU.mult)
                K.add("vector", f_nm, waits=[f"v_repev{lab}"], label=f"v_nm{lab}")

            pending_stages = [s_dnraw, s_r1, s_rts, s_rt8, s_rt8ev,
                              s_rep, s_repev, s_nm]
            last_repev[gp] = f"v_repev{lab}"
            prev_repev = f"v_repev{lab}"
            prev_norm = f"v_nm{lab}"
            g_idx += 1
    for st_ in pending_stages:
        st_()
    pending_stages = []

    if debug:
        K.add("gpsimd", lambda g: g.dma_start(out=dbg["d_at"][:, :], in_=attnO[:]),
              waits=["v_nm15_1"], label="dbg_at", sem="DO")

    _chk("p4")
    gwS = T2("gwS", (128, 128), BF16)
    ATs = T2("ATs", (128, 2048), BF16)
    lgS = T2("lgS", (8, OWN), F32)
    cwS = T2("cwS", (128, 64), F32)
    m1S = T2("m1S", (128, 8), F32)
    m2S = T2("m2S", (128, 8), F32)
    peS = T2("peS", (128, 64), F32)
    keepS = T2("keepS", (128, 64), F32)
    den2S = T2("den2S", (128, 8), F32)
    cwrepS = T2("cwrepS", (128, OWN), F32)
    cwTs = T2("cwTs", (8, OWN), F32)
    lgT = T2("lgT", (128, 64), F32)

    dma("gpsimd", gwS[:], gwTt[:, :], label="c_gw", waits=S1_LAST, sem="GL")
    dma("gpsimd", ATs[:], ATt_d[:, :], label="c_at", waits=S1_LAST, sem="GL")
    for _lbl in ("c_gw", "c_at"):
        K.labels[_lbl] = ("GL", K.cnt["GL"])

    def f_lg(t):
        ins = None
        for d2 in range(16):
            for ngo in range(2):
                ins = t.matmul(pB[0:8, ngo * 512:(ngo + 1) * 512],
                               gwS[:, d2 * 8:(d2 + 1) * 8],
                               xnO[:, d2 * 1024 + ngo * 512: d2 * 1024 + (ngo + 1) * 512],
                               start=(d2 == 0), stop=(d2 == 15))
        return ins
    K.add("tensor", f_lg, waits=["c_gw", prev_repev], label="t_lg")
    K.add("vector", lambda v: v.tensor_copy(lgS[:], pB[0:8, 0:1024]),
          waits=["t_lg"], label="v_lgev")
    _chk("g1")

    def f_lgt(t):
        ins = None
        for tc in range(8):
            ins = t.matmul(pC[:, 8 * tc:8 * tc + 8], lgS[:, tc * 128:(tc + 1) * 128],
                           I128f[0:8, 0:8], start=True, stop=True)
        return ins
    K.add("tensor", f_lgt, waits=["v_lgev", "c_i128f", prev_norm], label="t_lgt")
    K.add("vector", lambda v: v.tensor_copy(lgT[:], pC[:, 0:64]),
          waits=["t_lgt"], label="v_lgt")
    lg3 = lgT[:].rearrange("p (tc e) -> p tc e", e=8)
    K.add("vector", lambda v: v.reduce_max(m1S[:], lg3, axis=AX.X, negate=True),
          waits=["v_lgt"], label="v_m1")
    _chk("g2")

    def f_pe(s):
        ins = None
        for tc in range(8):
            ins = s.activation(peS[:, 8 * tc:8 * tc + 8], lgT[:, 8 * tc:8 * tc + 8],
                               AF.Exp, bias=m1S[:, tc:tc + 1])
        return ins
    K.add("scalar", f_pe, waits=["v_m1"], label="s_pe")
    _chk("g3")

    def f_eq(v):
        ins = None
        for tc in range(8):
            ins = v.tensor_scalar(keepS[:, 8 * tc:8 * tc + 8], lgT[:, 8 * tc:8 * tc + 8],
                                  m1S[:, tc:tc + 1], None, ALU.add)
        return ins
    K.add("vector", f_eq, waits=["v_m1"], label="v_eq0")
    K.add("vector", lambda v: v.tensor_single_scalar(keepS[:], keepS[:], 0.0, ALU.is_ge),
          waits=["v_eq0"], label="v_eq1")
    _chk("g3a")
    K.add("vector", lambda v: v.scalar_tensor_tensor(
        keepS[:], keepS[:], -1e30, lgT[:], ALU.mult, ALU.add),
        waits=["v_eq1"], label="v_p2")
    _chk("g3b")
    K.add("vector", lambda v: v.reduce_max(
        m2S[:], keepS[:].rearrange("p (tc e) -> p tc e", e=8), axis=AX.X),
        waits=["v_p2"], label="v_m2")
    _chk("g3c")

    def f_keep(v):
        ins = None
        for tc in range(8):
            ins = v.tensor_scalar(keepS[:, 8 * tc:8 * tc + 8], lgT[:, 8 * tc:8 * tc + 8],
                                  m2S[:, tc:tc + 1], None, ALU.subtract)
        return ins
    K.add("vector", f_keep, waits=["v_m2"], label="v_kp0")
    K.add("vector", lambda v: v.tensor_single_scalar(keepS[:], keepS[:], 0.0, ALU.is_ge),
          waits=["v_kp0"], label="v_kp1")
    K.add("vector", lambda v: v.tensor_tensor(peS[:], peS[:], keepS[:], ALU.mult),
          waits=["v_kp1", "s_pe"], label="v_kp2")
    _chk("g3d")
    K.add("vector", lambda v: v.reduce_sum(
        den2S[:], peS[:].rearrange("p (tc e) -> p tc e", e=8), axis=AX.X),
        waits=["v_kp2"], label="v_den2")
    K.add("vector", lambda v: v.reciprocal(m1S[:], den2S[:]),
          waits=["v_den2"], label="v_rcd")
    _chk("g3e")

    def f_cw(v):
        ins = None
        for tc in range(8):
            ins = v.tensor_scalar(cwS[:, 8 * tc:8 * tc + 8], peS[:, 8 * tc:8 * tc + 8],
                                  m1S[:, tc:tc + 1], None, ALU.mult)
        return ins
    K.add("vector", f_cw, waits=["v_rcd"], label="v_cw")
    _chk("g4")
    if debug:
        K.add("gpsimd", lambda g: g.dma_start(out=dbg["d_cw"][:, :], in_=cwS[:]),
              waits=["v_cw"], label="dbg_cw", sem="DO")

    for tc in range(8):
        w = ["v_cw", "c_i128f"]
        w.append(prev_norm if tc == 0 else f"v_cwtev{tc-1}")
        K.add("tensor", lambda t, tc=tc: t.matmul(
            pD[0:8, 0:128], cwS[:, 8 * tc:8 * tc + 8], I128f[:], start=True, stop=True),
            waits=w, label=f"t_cwt{tc}")
        K.add("vector", lambda v, tc=tc: v.tensor_copy(
            cwTs[:, tc * 128:(tc + 1) * 128], pD[0:8, 0:128]),
            waits=[f"t_cwt{tc}"], label=f"v_cwtev{tc}")
        if tc == 7:
            _chk("g5")

    def f_cwr(t):
        ins = None
        for ngo in range(2):
            ins = t.matmul(pB[:, ngo * 512:(ngo + 1) * 512], RexpS[:],
                           cwTs[:, ngo * 512:(ngo + 1) * 512], start=True, stop=True)
        return ins
    K.add("tensor", f_cwr, waits=["v_cwtev7", "c_rexp", "v_lgev"], label="t_cwr")
    K.add("vector", lambda v: v.tensor_copy(cwrepS[:], pB[:, 0:1024]),
          waits=["t_cwr"], label="v_cwrev")
    _chk("g6")

    def f_ta(t):
        ins = None
        for d2 in range(16):
            for ngo in range(2):
                ins = t.matmul(pA[:, ngo * 512:(ngo + 1) * 512],
                               ATs[:, d2 * 128:(d2 + 1) * 128],
                               xnO[:, d2 * 1024 + ngo * 512: d2 * 1024 + (ngo + 1) * 512],
                               start=(d2 == 0), stop=(d2 == 15))
        return ins
    K.add("tensor", f_ta, waits=["c_at", f"s_ex{kci-1}"], label="t_ta")
    K.add("vector", lambda v: v.tensor_tensor(tsS[:], pA[:, 0:1024], cwrepS[:], ALU.mult),
          waits=["t_ta", "v_cwrev"], label="v_ts")
    if debug:
        K.add("gpsimd", lambda g: g.dma_start(out=dbg["d_ts"][:, :], in_=tsS[:]),
              waits=["v_ts"], label="dbg_ts", sem="DO")

    _chk("p5pre")
    s2_cx.close()
    kv_cx.close()
    open_stacks.remove(s2_cx)
    open_stacks.remove(kv_cx)

    s3_cx = contextlib.ExitStack()
    open_stacks.append(s3_cx)
    T3 = lambda nm, shp, dt: s3_cx.enter_context(nc.sbuf_tensor(nm, list(shp), dt))
    BTs = T3("BTs", (128, 2048), BF16)
    fwS = [T3(f"fwS{i}", (128, 8192), BF16) for i in range(2)]
    owS = [T3(f"owS{i}", (128, 8192), BF16) for i in range(2)]
    gS = T3("gS", (128, 512), BF16)
    sgS = T3("sgS", (128, 512), F32)
    xoS = [T3(f"xoS{i}", (128, 512), F32) for i in range(2)]
    yS = [T3(f"yS{i}", (128, 512), F32) for i in range(2)]

    ydst = y_out.rearrange("(n p) d -> n p d", p=128)
    xo_src = x_in.rearrange("(n p) d -> n p d", p=128)

    S2_LAST = ["v_ts", "v_nm15_1"]
    if debug and not DEBUG_NOGATE:
        S2_LAST.append(("DO", K.cnt["DO"]))
    dma("gpsimd", BTs[:], BTt_d[:, :], label="c_bt", waits=list(S2_LAST), sem="GL")
    for og in range(4):
        w = list(S2_LAST) if og == 0 else [f"t_o{og-1}_7"]
        dma("gpsimd", fwS[og % 2][:], fwTt[og], label=f"dg_fw{og}", waits=w, sem="SLAB")
        dma("gpsimd", owS[og % 2][:], owTt[og], label=f"dg_ow{og}", waits=w, sem="SLAB2")
        for tc in range(8):
            it = og * 8 + tc
            par = it % 2
            if it >= 2:
                w = [f"v_y{og}_{tc-2}"] if tc >= 2 else [f"v_y{og-1}_{6+tc}"]
            else:
                w = list(S2_LAST)
            dma("sync", xoS[par][:], xo_src[8 + tc][:, og * 512:(og + 1) * 512],
                label=f"ds_xo{og}_{tc}", waits=w, sem="XO")

            def f_fd(t, og=og, tc=tc, par=par):
                for d2 in range(16):
                    t.matmul(pB[:, par * 512:(par + 1) * 512],
                             xnO[:, d2 * 1024 + tc * 128: d2 * 1024 + (tc + 1) * 128],
                             fwS[og % 2][:, d2 * 512:(d2 + 1) * 512],
                             start=(d2 == 0), stop=False)
                return t.matmul(pB[:, par * 512:(par + 1) * 512],
                                tsS[:, tc * 128:(tc + 1) * 128],
                                BTs[:, og * 512:(og + 1) * 512],
                                start=False, stop=True)
            w = [("SLAB", 16 * (og + 1)), "c_bt", "v_ts"]
            if it >= 2:
                w.append(f"s_g{og}_{tc-2}" if tc >= 2 else f"s_g{og-1}_{6+tc}")
            else:
                w.append("v_cwrev")
            K.add("tensor", f_fd, waits=w, label=f"t_fd{og}_{tc}")

            def f_o(t, og=og, tc=tc, par=par):
                ins = None
                for c2 in range(16):
                    ins = t.matmul(pA[:, par * 512:(par + 1) * 512],
                                   attnO[:, c2 * 1024 + tc * 128: c2 * 1024 + (tc + 1) * 128],
                                   owS[og % 2][:, c2 * 512:(c2 + 1) * 512],
                                   start=(c2 == 0), stop=(c2 == 15))
                return ins
            w = [("SLAB2", 16 * (og + 1))]
            if it >= 2:
                w.append(f"v_sg{og}_{tc-2}" if tc >= 2 else f"v_sg{og-1}_{6+tc}")
            else:
                w.append("v_ts")
            K.add("tensor", f_o, waits=w, label=f"t_o{og}_{tc}")

            def f_g(s, og=og, par=par):
                return s.activation(gS[:], pB[:, par * 512:(par + 1) * 512], AF.Silu)
            w = [f"t_fd{og}_{tc}"]
            if it >= 1:
                w.append(f"v_sg{og}_{tc-1}" if tc >= 1 else f"v_sg{og-1}_7")
            K.add("scalar", f_g, waits=w, label=f"s_g{og}_{tc}")

            def f_sg(v, par=par):
                return v.tensor_tensor(sgS[:], pA[:, par * 512:(par + 1) * 512],
                                       gS[:], ALU.mult)
            w = [f"s_g{og}_{tc}", f"t_o{og}_{tc}"]
            if it >= 1:
                w.append(f"v_y{og}_{tc-1}" if tc >= 1 else f"v_y{og-1}_7")
            K.add("vector", f_sg, waits=w, label=f"v_sg{og}_{tc}")

            def f_y(v, par=par):
                return v.tensor_tensor(yS[par][:], sgS[:], xoS[par][:], ALU.add)
            w = [f"v_sg{og}_{tc}", ("XO", 16 * (it + 1))]
            if it >= 2:
                w.append(("YD", 16 * it))
            K.add("vector", f_y, waits=w, label=f"v_y{og}_{tc}")
            dma("sync", ydst[tc][:, og * 512:(og + 1) * 512], yS[par][:],
                label=f"ds_y{og}_{tc}", waits=[f"v_y{og}_{tc}"], sem="YD")

    n_yd = K.cnt["YD"]
    K.raw("sync", lambda e: e.wait_ge(sems["YD"], n_yd))
    if debug:
        n_do = K.cnt["DO"]
        K.raw("sync", lambda e: e.wait_ge(sems["DO"], n_do))
    if STOP_AFTER is not None:
        mk = _marks[STOP_AFTER]
        for e in K.steps:
            K.steps[e] = K.steps[e][:mk[e]]
        wsem, wval = K.labels[_stop_label[STOP_AFTER]]
        K.raw("sync", lambda e: e.wait_ge(sems[wsem], wval))
    _emit_and_close(nc, K, sems, list(reversed(open_stacks)))
    return nc


def _emit_and_close(nc, K, sems, stacks):
    with nc.Block() as block:
        @block.tensor
        def _(t):
            K.emit("tensor", t, sems)

        @block.vector
        def _(v):
            K.emit("vector", v, sems)

        @block.scalar
        def _(s):
            K.emit("scalar", s, sems)

        @block.gpsimd
        def _(g):
            K.emit("gpsimd", g, sems)

        @block.sync
        def _(sy):
            K.emit("sync", sy, sems)
    for st_ in stacks:
        st_.close()




_NC_CACHE = {}


def kernel(**inputs):
    in_maps = _prep_inputs(inputs)
    key = ("main", DEBUG)
    if key not in _NC_CACHE:
        _NC_CACHE[key] = build_nc(debug=DEBUG)
    nc = _NC_CACHE[key]
    res = run_bass_kernel_spmd(nc, in_maps, core_ids=list(range(NCORES)))
    y = np.empty((4, S, D), np.float32)
    for c in range(NCORES):
        b, h = divmod(c, 2)
        blocks = ([0, 3] if h == 0 else [1, 2])
        yc = res.results[c]["y"]
        for i, p in enumerate(blocks):
            y[b, 512 * p:512 * (p + 1), :] = yc[512 * i:512 * (i + 1)]
    return y


# version 21
# speedup vs baseline: 1.0138x; 1.0138x over previous
import math
import contextlib
import numpy as np
import concourse.bass as bass
import concourse.mybir as mybir
from concourse.bass_utils import run_bass_kernel_spmd

F32 = mybir.dt.float32
BF16 = mybir.dt.bfloat16
I32 = mybir.dt.int32
F16 = mybir.dt.float16
AF = mybir.ActivationFunctionType
ALU = mybir.AluOpType
AX = mybir.AxisListType

S = 2048
OWN = 1024
D = 2048
NCORES = 8
DEBUG = False
DEBUG_NOGATE = False
STOP_AFTER = None


class _StopBuild(Exception):
    pass

TWO_PI = 2.0 * math.pi
HALF_PI = math.pi / 2.0
LN1E4_32 = math.log(10000.0) / 32.0


def _headpair(c2):
    m, j = divmod(c2, 4)
    return 8 * m + j, 8 * m + 4 + j


class Sched:
    SEMS = ("T", "V", "Sc", "G", "SY", "XD", "XO", "YD", "GC", "GL", "PQ", "KRD", "WQ", "WQ2", "QF", "QF2", "SLAB", "SLAB2", "DO")

    def __init__(self):
        self.steps = {e: [] for e in ("tensor", "vector", "scalar", "gpsimd", "sync")}
        self.cnt = {s: 0 for s in self.SEMS}
        self.labels = {}

    DMA_SEMS = ("SY", "XD", "XO", "YD", "GC", "GL", "PQ", "KRD", "WQ", "WQ2", "QF", "QF2", "SLAB", "SLAB2", "DO")

    def add(self, eng, fn, waits=(), label=None, sem=None):
        if sem is None:
            sem = {"tensor": "T", "vector": "V", "scalar": "Sc", "gpsimd": "G"}[eng]
        inc = 16 if sem in self.DMA_SEMS else 1
        do_inc = label is not None
        if do_inc:
            self.cnt[sem] += inc
            self.labels[label] = (sem, self.cnt[sem])
        for w in waits:
            assert isinstance(w, tuple) or w in self.labels, f"unknown label {w!r}"
        self.steps[eng].append((fn, tuple(waits), sem if do_inc else None, inc))

    def raw(self, eng, fn):
        self.steps[eng].append((fn, (), None, 0))

    def emit(self, eng_name, eng, sems):
        hw = {}
        for fn, waits, sem, inc in self.steps[eng_name]:
            for w in waits:
                wsem, wval = w if isinstance(w, tuple) else self.labels[w]
                if hw.get(wsem, -1) < wval:
                    eng.wait_ge(sems[wsem], wval)
                    hw[wsem] = wval
            ins = fn(eng)
            if sem is not None:
                assert ins is not None
                ins.then_inc(sems[sem], inc)


def _perm_order():
    order = []
    for c2 in range(16):
        ht, hb = _headpair(c2)
        order += list(range(64 * ht, 64 * ht + 64))
        order += list(range(64 * hb, 64 * hb + 64))
    return np.asarray(order)


def _consts():
    c = {}
    c["I128"] = np.eye(128, dtype=np.float32)
    c["I8"] = np.eye(8, dtype=np.float32)
    kk, qq = np.meshgrid(np.arange(128), np.arange(128), indexing="ij")
    c["maskD"] = (kk <= qq).astype(np.float32)
    ma = np.zeros((4, 128, 512), np.float32)
    for i in range(4):
        kp, qp = np.meshgrid(np.arange(128), np.arange(512), indexing="ij")
        ma[i] = np.where(qp < 128 * i + kp, -1e9, 0.0)
    c["maskAdd"] = np.ascontiguousarray(ma.transpose(1, 0, 2).reshape(128, 4 * 512))
    c["permQ"] = np.zeros((128, 128), np.float32)
    c["permK"] = np.zeros((128, 128), np.float32)
    for p2 in range(128):
        c["permQ"][p2 ^ 32, p2] = 1.0
        c["permK"][p2 ^ 16, p2] = 1.0
    for par in range(2):
        selR = np.zeros((128, 128), np.float32)
        selN = np.zeros((128, 128), np.float32)
        for half in range(2):
            lh = 2 * par + half
            for d in range(64):
                pp = 64 * half + d
                if d < 16:
                    selR[32 * lh + d, pp] = 1.0
                elif d < 32:
                    selN[32 * lh + (d - 16), pp] = 1.0
                elif d < 48:
                    selR[32 * lh + 16 + (d - 32), pp] = 1.0
                else:
                    selN[32 * lh + 16 + (d - 48), pp] = 1.0
        c[f"selR{par}"] = selR
        c[f"selN{par}"] = selN
    rexp = np.zeros((8, 128), np.float32)
    for e in range(8):
        rexp[e, 16 * e:16 * e + 16] = 16.0
    c["Rexp"] = rexp
    c["onesc"] = np.ones((128, 1), np.float32)
    e2 = np.zeros((65, 64), np.float32)
    e2[0] = np.eye(8, dtype=np.float32).reshape(64)
    e2[64] = e2[0]
    c["E2"] = e2
    sels = np.zeros((8, 4, 128), np.float32)
    for s_ in range(4):
        for pp in range(128):
            sels[4 * (pp // 64) + s_, s_, pp] = 1.0
    c["SelS"] = np.ascontiguousarray(sels.transpose(0, 1, 2).reshape(8, 512))
    p = np.arange(128)
    c["signQ"] = np.where((p % 64) < 32, -1.0, 1.0).astype(np.float32).reshape(128, 1)
    c["signK"] = np.where((p % 32) < 16, -1.0, 1.0).astype(np.float32).reshape(128, 1)
    c["imodQ"] = (p % 32).astype(np.float32).reshape(128, 1)
    c["imodK"] = (p % 16).astype(np.float32).reshape(128, 1)
    return c


def _tile_w(wT, gsz):
    n = wT.shape[1] // gsz
    t = wT.reshape(16, 128, n, gsz).transpose(2, 1, 0, 3)
    return np.ascontiguousarray(t.reshape(n, 128, 16 * gsz))


def _prep_inputs(inp):
    x = np.asarray(inp["L0_attn_hidden_states"], np.float32)
    qperm = _perm_order()
    qwT = np.ascontiguousarray(np.asarray(inp["L0_attn_q_w"], np.float32)[qperm, :].T)
    owT = np.ascontiguousarray(np.asarray(inp["L0_attn_o_w"], np.float32).T[qperm, :])
    fwT = np.ascontiguousarray(np.asarray(inp["L0_ffn_weight"], np.float32).T)
    krT = np.asarray(inp["L0_attn_kr_w"], np.float32).T
    dwT = np.asarray(inp["L0_attn_down_w"], np.float32).T
    upkT = np.asarray(inp["L0_attn_upk_w"], np.float32).T
    upvT = np.asarray(inp["L0_attn_upv_w"], np.float32).T
    gwT = np.asarray(inp["L0_ffn_gate_w"], np.float32).T
    AT = np.ascontiguousarray(np.asarray(inp["L0_ffn_A_w"], np.float32).reshape(128, 2048).T)
    BT = np.ascontiguousarray(np.asarray(inp["L0_ffn_B_w"], np.float32).transpose(0, 2, 1).reshape(128, 2048))

    shared = dict(
        qwTt=_tile_w(qwT, 128),
        fwTt=_tile_w(fwT, 512),
        owTt=np.ascontiguousarray(
            owT.reshape(16, 128, 4, 512).transpose(2, 1, 0, 3).reshape(4, 128, 8192)),
        krTt=np.ascontiguousarray(krT.reshape(16, 128, 256).transpose(1, 0, 2).reshape(128, 4096)),
        dwTt=np.ascontiguousarray(dwT.reshape(16, 128, 256).transpose(1, 0, 2).reshape(128, 4096)),
        upkTt=np.ascontiguousarray(upkT.reshape(2, 128, 256).transpose(1, 0, 2).reshape(128, 512)),
        upvTt=np.ascontiguousarray(upvT.reshape(2, 128, 512).transpose(1, 0, 2).reshape(128, 1024)),
        gwTt=np.ascontiguousarray(gwT.reshape(16, 128, 8).transpose(1, 0, 2).reshape(128, 128)),
        ATt=np.ascontiguousarray(AT.reshape(16, 128, 128).transpose(1, 0, 2).reshape(128, 2048)),
        BTt=BT,
    )
    shared.update(_consts())

    in_maps = []
    for c in range(NCORES):
        b, h = divmod(c, 2)
        blocks = ([1, 2, 0, 3] if h == 0 else [0, 3, 1, 2])
        x_in = np.ascontiguousarray(
            np.concatenate([x[b, 512 * p:512 * (p + 1)] for p in blocks], axis=0))
        bias_kill = np.full((128, 32), -10.0, np.float32)
        if h == 0:
            bias_kill[:, 0:4] = -1e30
        else:
            bias_kill[:, 16 + 4:16 + 8] = -1e30
        poff = np.zeros((128, 8), np.float32)
        poff[:, 0:4] = np.array([512.0 * p for p in blocks], np.float32)[None, :]
        poff[:, 4:6] = np.array([512.0 * p for p in blocks[2:]], np.float32)[None, :]
        d = dict(shared)
        d["x_in"] = x_in
        d["biasKill"] = bias_kill
        d["poff"] = poff
        in_maps.append(d)
    return in_maps


def build_nc(debug=False):
    nc = bass.Bass("TRN2", target_bir_lowering=False)

    def din(name, shape):
        return nc.dram_tensor(name, list(shape), F32, kind="ExternalInput").ap()

    x_in = din("x_in", (S, D))
    qwTt = din("qwTt", (16, 128, 2048))
    fwTt = din("fwTt", (4, 128, 8192))
    owTt = din("owTt", (4, 128, 8192))
    krTt = din("krTt", (128, 4096))
    dwTt = din("dwTt", (128, 4096))
    upkTt = din("upkTt", (128, 512))
    upvTt = din("upvTt", (128, 1024))
    gwTt = din("gwTt", (128, 128))
    ATt_d = din("ATt", (128, 2048))
    BTt_d = din("BTt", (128, 2048))
    I128_d = din("I128", (128, 128))
    I8_d = din("I8", (8, 8))
    maskD_d = din("maskD", (128, 128))
    maskAdd_d = din("maskAdd", (128, 2048))
    permQ_d = din("permQ", (128, 128))
    permK_d = din("permK", (128, 128))
    selR_d = [din(f"selR{p}", (128, 128)) for p in range(2)]
    selN_d = [din(f"selN{p}", (128, 128)) for p in range(2)]
    E2_d = din("E2", (65, 64))
    SelS_d = din("SelS", (8, 512))
    Rexp_d = din("Rexp", (8, 128))
    onesc_d = din("onesc", (128, 1))
    signQ_d = din("signQ", (128, 1))
    signK_d = din("signK", (128, 1))
    imodQ_d = din("imodQ", (128, 1))
    imodK_d = din("imodK", (128, 1))
    biasKill_d = din("biasKill", (128, 32))
    poff_d = din("poff", (128, 8))

    y_out = nc.dram_tensor("y", [OWN, D], F32, kind="ExternalOutput").ap()
    dbg = {}
    if debug:
        for nm, shp in [("d_xn", (128, 32768)), ("d_k", (128, 4 * S)),
                        ("d_v", (128, 16 * 520)), ("d_q", (128, 16 * OWN)),
                        ("d_at", (128, 16 * OWN)), ("d_cw", (128, 64)),
                        ("d_ts", (128, OWN))]:
            dbg[nm] = nc.dram_tensor(nm, list(shp), F32, kind="ExternalOutput").ap()

    K = Sched()
    cx = contextlib.ExitStack()
    T = lambda nm, shp, dt: cx.enter_context(nc.sbuf_tensor(nm, list(shp), dt))

    xnO = T("xnO", (128, 16 * OWN), BF16)
    attnO = T("attnO", (128, 16 * OWN), BF16)
    cosQ = T("cosQ", (128, OWN), BF16)
    sinQ = T("sinQ", (128, OWN), BF16)
    repS = T("repS", (128, 512), BF16)
    tsS = T("tsS", (128, OWN), BF16)
    I128 = T("I128s", (128, 128), BF16)
    onesb = T("onesb", (128, 1), BF16)
    biasK = T("biasKs", (128, 32), F32)
    csc = T("csc", (128, 4), F32)
    stat6 = T("stat6", (128, 48), F32)
    aggr2 = T("aggr2", (128, 8), F32)
    nmrS = T("nmrS", (128, 2), F32)
    invQ = T("invQ", (128, 1), F32)
    invK = T("invK", (128, 1), F32)
    signQs = T("signQs", (128, 1), F32)
    signKs = T("signKs", (128, 1), F32)
    imodQs = T("imodQs", (128, 1), F32)
    imodKs = T("imodKs", (128, 1), F32)
    poffS = T("poffS", (128, 8), F32)
    CC_M5, CC_EPS, CC_125 = 0, 1, 2

    pA = cx.enter_context(nc.psum_tensor("pA", [128, 2048], F32))
    pB = cx.enter_context(nc.psum_tensor("pB", [128, 1024], F32))
    pC = cx.enter_context(nc.psum_tensor("pC", [128, 512], F32))
    pD = cx.enter_context(nc.psum_tensor("pD", [128, 512], F32))

    sems = {s_: cx.enter_context(nc.semaphore(s_)) for s_ in Sched.SEMS}

    open_stacks = [cx]
    kv_cx = contextlib.ExitStack()
    open_stacks.append(kv_cx)
    Tk = lambda nm, shp, dt: kv_cx.enter_context(nc.sbuf_tensor(nm, list(shp), dt))
    kT = Tk("kT", (128, 4 * S), BF16)
    vT = Tk("vT", (128, 16 * 520), BF16)

    s1_cx = contextlib.ExitStack()
    open_stacks.append(s1_cx)
    T1 = lambda nm, shp, dt: s1_cx.enter_context(nc.sbuf_tensor(nm, list(shp), dt))
    xnP = T1("xnP", (128, 16 * 1024), BF16)
    xbuf = [T1(f"xbuf{i}", (128, D), F32) for i in range(2)]
    xnt = T1("xnt", (128, D), BF16)
    iotaB = T1("iotaB", (128, S), F16)
    scr0 = T1("scr0", (128, 512), F32)
    scr1 = T1("scr1", (128, 512), F32)
    cosK = T1("cosK", (128, S), BF16)
    sinK = T1("sinK", (128, S), BF16)
    krA = T1("krA", (128, 4096), BF16)
    dwS = T1("dwS", (128, 4096), BF16)
    upk = T1("upk", (128, 512), BF16)
    upv = T1("upv", (128, 1024), BF16)
    ckvS = T1("ckvS", (128, 2 * S), BF16)
    rkr = T1("rkr", (128, 512), BF16)
    kcs = T1("kcs", (128, 512), BF16)
    rop0 = T1("rop0", (128, 512), BF16)
    rop1 = T1("rop1", (128, 512), BF16)
    permKs = T1("permKs", (128, 128), BF16)
    selRs = [T1(f"selRs{p}", (128, 128), BF16) for p in range(2)]
    selNs = [T1(f"selNs{p}", (128, 128), BF16) for p in range(2)]

    def dma(eng_name, dst, src, label=None, waits=(), sem="SY"):
        K.add(eng_name, lambda e, d=dst, s=src: e.dma_start(out=d, in_=s),
              waits=waits, label=label, sem=sem)

    dma("sync", biasK[:], biasKill_d[:, :], label="c_bk")
    dma("sync", signQs[:], signQ_d[:, :], label="c_sq")
    dma("sync", signKs[:], signK_d[:, :], label="c_sk")
    dma("sync", imodQs[:], imodQ_d[:, :], label="c_iq")
    dma("sync", imodKs[:], imodK_d[:, :], label="c_ik")
    dma("sync", poffS[:], poff_d[:, :], label="c_poff")
    dma("gpsimd", I128[:], I128_d[:, :], label="c_i128", sem="GC")
    dma("gpsimd", onesb[:], onesc_d[:, :], label="c_ones", sem="GC")
    dma("gpsimd", permKs[:], permK_d[:, :], label="c_pk", sem="GC")
    for p in range(2):
        dma("gpsimd", selRs[p][:], selR_d[p][:, :], label=f"c_selr{p}", sem="GC")
        dma("gpsimd", selNs[p][:], selN_d[p][:, :], label=f"c_seln{p}", sem="GC")
    dma("gpsimd", upk[:], upkTt[:, :], label="c_upk", sem="GC")
    dma("gpsimd", upv[:], upvTt[:, :], label="c_upv", sem="GC")
    dma("gpsimd", krA[:], krTt[:, :], label="c_kra", sem="GC")
    dma("gpsimd", dwS[:], dwTt[:, :], label="c_dws", sem="GC")
    for _lbl in ("c_bk", "c_sq", "c_sk", "c_iq", "c_ik", "c_poff"):
        K.labels[_lbl] = ("SY", 6 * 16)
    for _lbl in ("c_i128", "c_ones", "c_pk", "c_selr0",
                 "c_seln0", "c_selr1", "c_seln1", "c_upk", "c_upv", "c_kra",
                 "c_dws"):
        K.labels[_lbl] = ("GC", K.cnt["GC"])

    def f_csc(v):
        v.memset(csc[:, CC_M5:CC_M5 + 1], -0.5)
        v.memset(csc[:, CC_EPS:CC_EPS + 1], 1e-5)
        v.memset(csc[:, CC_125:CC_125 + 1], 0.125)
        return v.memset(
            vT[:].rearrange("p (b c) -> p b c", c=65)[:, :, 64:65], 1.0)
    K.add("vector", f_csc, label="v_csc")
    K.add("gpsimd", lambda g: g.iota(iotaB[:], [[1, S]], base=0, channel_multiplier=0,
                                     allow_small_or_imprecise_dtypes=True),
          label="g_iota")
    _stop_label = {"p1": "v_ev15_3", "p2": "v_vev15",
                   "p3": "v_qr3_15", "p4": "v_nm15_1", "p5pre": "v_ts",
                   "g1": "v_lgev", "g2": "v_m1", "g3": "s_pe", "g4": "v_cw",
                   "g5": "v_cwtev7", "g6": "v_cwrev",
                   "g3a": "v_eq1", "g3b": "v_p2", "g3c": "v_m2",
                   "g3d": "v_kp2", "g3e": "v_rcd"}
    _marks = {}
    def _chk(tag):
        _marks[tag] = {e: len(K.steps[e]) for e in K.steps}

    def table(pref, imod_t, sign_t, inv_t, cosT, sinT, ncols, prev_last):
        units = []
        unit = [("vector",
                 lambda v: v.tensor_single_scalar(inv_t[:], imod_t[:], -LN1E4_32, ALU.mult),
                 ["c_iq" if pref == "Q" else "c_ik", "c_poff",
                  "c_sq" if pref == "Q" else "c_sk"],
                 f"v_{pref}inv0"),
                ("scalar",
                 lambda s: s.activation(inv_t[:], inv_t[:], AF.Exp),
                 [f"v_{pref}inv0"], f"s_{pref}inv")]
        half = 512
        last = prev_last
        for hh in range(ncols // 512):
            for trig in range(2):
                tag = f"{pref}{hh}{trig}"
                pcol = (4 + hh) if pref == "Q" else hh
                chain = [
                    lambda v, half=half, pcol=pcol: v.tensor_scalar(
                        scr0[:, 0:half], iotaB[:, 0:half], poffS[:, pcol:pcol + 1],
                        inv_t[:, 0:1], ALU.add, ALU.mult),
                ]
                if trig == 1:
                    chain.append(lambda v, half=half: v.tensor_single_scalar(
                        scr0[:, 0:half], scr0[:, 0:half], HALF_PI, ALU.add))
                chain += [
                    lambda v, half=half: v.tensor_single_scalar(
                        scr1[:, 0:half], scr0[:, 0:half], 1.0 / TWO_PI, ALU.mult),
                    lambda v, half=half: v.tensor_copy(
                        scr1[:, 0:half].bitcast(I32), scr1[:, 0:half]),
                    lambda v, half=half: v.tensor_copy(
                        scr1[:, 0:half], scr1[:, 0:half].bitcast(I32)),
                    lambda v, half=half: v.scalar_tensor_tensor(
                        scr0[:, 0:half], scr1[:, 0:half], -TWO_PI, scr0[:, 0:half],
                        ALU.mult, ALU.add),
                    lambda v, half=half: v.tensor_single_scalar(
                        scr1[:, 0:half], scr0[:, 0:half], math.pi, ALU.is_gt),
                    lambda v, half=half: v.scalar_tensor_tensor(
                        scr0[:, 0:half], scr1[:, 0:half], -TWO_PI, scr0[:, 0:half],
                        ALU.mult, ALU.add),
                    lambda v, half=half: v.tensor_single_scalar(
                        scr0[:, 0:half], scr0[:, 0:half], math.pi, ALU.min),
                    lambda v, half=half: v.tensor_single_scalar(
                        scr0[:, 0:half], scr0[:, 0:half], -math.pi, ALU.max),
                ]
                prev = None
                for i, st in enumerate(chain):
                    w = [prev] if prev else ([f"s_{pref}inv", "g_iota"] +
                                             ([last] if last else []))
                    unit.append(("vector", st, w, f"v_rr{tag}_{i}"))
                    prev = f"v_rr{tag}_{i}"
                unit.append(("scalar", lambda s, half=half: s.activation(
                    scr1[:, 0:half], scr0[:, 0:half], AF.Sin),
                    [prev], f"s_sin{tag}"))
                cols = slice(hh * half, (hh + 1) * half)
                if trig == 0:
                    unit.append(("vector", lambda v, cols=cols, half=half, dstT=sinT: v.tensor_scalar(
                        dstT[:, cols], scr1[:, 0:half], sign_t[:, 0:1], None, ALU.mult),
                        [f"s_sin{tag}"], f"v_tab{tag}"))
                else:
                    unit.append(("vector", lambda v, cols=cols, half=half, dstT=cosT: v.tensor_copy(
                        dstT[:, cols], scr1[:, 0:half]),
                        [f"s_sin{tag}"], f"v_tab{tag}"))
                last = f"v_tab{tag}"
                units.append(unit)
                unit = []
        return units, last

    xsrc = x_in.rearrange("(n p) d -> n p d", p=128)
    for sc in range(16):
        bsel = xbuf[sc % 2]
        w = [f"v_xnt{sc-2}"] if sc >= 2 else []
        dma("sync", bsel[:], xsrc[sc], label=f"ds_x{sc}", waits=w, sem="XD")

        pp = sc % 2
        a0 = 4 * pp

        def f_st(v, b=bsel, pp=pp):
            ins = None
            for cchunk in range(4):
                ins = v.bn_stats(stat6[:, 24 * pp + cchunk * 6:24 * pp + cchunk * 6 + 6],
                                 b[:, cchunk * 512:(cchunk + 1) * 512])
            return ins
        K.add("vector", f_st,
              waits=[("XD", 16 * min(sc + 2, 16)), "v_csc"] +
                    ([f"v_ag{sc-2}"] if sc >= 2 else []),
              label=f"v_st{sc}")
        K.add("vector", lambda v, pp=pp, a0=a0: v.bn_aggr(
            aggr2[:, a0:a0 + 2], stat6[:, 24 * pp:24 * pp + 24]),
              waits=[f"v_st{sc}"] + ([f"s_ln{sc-2}", f"v_nmr{sc-2}"] if sc >= 2 else []),
              label=f"v_ag{sc}")
        K.add("scalar", lambda s, a0=a0: s.activation(
            aggr2[:, a0 + 2:a0 + 3], aggr2[:, a0 + 1:a0 + 2], AF.Ln,
            bias=csc[:, CC_EPS:CC_EPS + 1]),
              waits=[f"v_ag{sc}"] + ([f"s_rstd{sc-2}"] if sc >= 2 else []),
              label=f"s_ln{sc}")
        K.add("scalar", lambda s, a0=a0: s.activation(
            aggr2[:, a0 + 3:a0 + 4], aggr2[:, a0 + 2:a0 + 3], AF.Exp,
            scale=csc[:, CC_M5:CC_M5 + 1]),
              waits=[f"s_ln{sc}"] + ([f"v_xnt{sc-2}"] if sc >= 2 else []),
              label=f"s_rstd{sc}")
        K.add("vector", lambda v, pp=pp, a0=a0: v.scalar_tensor_tensor(
            nmrS[:, pp:pp + 1], aggr2[:, a0:a0 + 1], -1.0, aggr2[:, a0 + 3:a0 + 4],
            ALU.mult, ALU.mult),
            waits=[f"s_rstd{sc}"], label=f"v_nmr{sc}")

        def f_xnt(sC, b=bsel, pp=pp, a0=a0):
            return sC.activation(xnt[:], b[:], AF.Identity,
                                 bias=nmrS[:, pp:pp + 1],
                                 scale=aggr2[:, a0 + 3:a0 + 4])
        K.add("scalar", f_xnt,
              waits=[f"v_nmr{sc}"] + ([f"v_ev{sc-1}_3"] if sc >= 1 else []),
              label=f"v_xnt{sc}")
        for g in range(4):
            pt = pC if g % 2 == 0 else pD

            def f_tr(t, g=g, pt=pt):
                ins = None
                for jj in range(4):
                    d2 = 4 * g + jj
                    ins = t.matmul(pt[:, jj * 128:(jj + 1) * 128],
                                   xnt[:, d2 * 128:(d2 + 1) * 128], I128[:],
                                   start=True, stop=True)
                return ins
            w = [f"v_xnt{sc}", "c_i128"]
            if sc > 0 or g >= 2:
                pg = (sc, g - 2) if g >= 2 else (sc - 1, g + 2)
                w.append("v_ev%d_%d" % pg)
            K.add("tensor", f_tr, waits=w, label=f"t_tr{sc}_{g}")

            def f_ev(v, sc=sc, g=g, pt=pt):
                dst = xnP if sc < 8 else xnO
                col = (sc % 8) * 128
                return v.tensor_copy(
                    dst[:].rearrange("p (d2 c) -> p d2 c", c=1024)[:, 4 * g:4 * g + 4, col:col + 128],
                    pt[:].rearrange("p (j c) -> p j c", c=128))
            K.add("vector", f_ev, waits=[f"t_tr{sc}_{g}"], label=f"v_ev{sc}_{g}")

    _chk("p1")

    tabK, tk_done = table("K", imodKs, signKs, invK, cosK, sinK, S, None)
    tabQ, tq_done = table("Q", imodQs, signQs, invQ, cosQ, sinQ, OWN, tk_done)

    def emit_units(us):
        for u in us:
            for eng, fn, w, lab in u:
                K.add(eng, fn, waits=w, label=lab)

    def xn_mov(d2, ng):
        if ng < 2:
            return xnP[:, d2 * 1024 + ng * 512: d2 * 1024 + (ng + 1) * 512]
        return xnO[:, d2 * 1024 + (ng - 2) * 512: d2 * 1024 + (ng - 1) * 512]

    def xn_ready(ng):
        return [f"v_ev{sc}_{g}" for sc in range(4 * ng, 4 * ng + 4) for g in range(4)]

    for ng in range(4):
        for d2 in range(16):
            lin = ng * 16 + d2

            def f_mm(t, ng=ng, d2=d2, lin=lin):
                st, sp = d2 == 0, d2 == 15
                ins = None
                for cc in range(2):
                    t.matmul(pA[:, cc * 512:(cc + 1) * 512],
                             krA[:, d2 * 256 + cc * 128: d2 * 256 + (cc + 1) * 128],
                             xn_mov(d2, ng), start=st, stop=sp)
                for cc in range(2):
                    ins = t.matmul(pA[:, (2 + cc) * 512:(3 + cc) * 512],
                                   dwS[:, d2 * 256 + cc * 128: d2 * 256 + (cc + 1) * 128],
                                   xn_mov(d2, ng), start=st, stop=sp)
                return ins
            w = ["c_kra", "c_dws"]
            if d2 == 0:
                w += xn_ready(ng)
                if ng > 0:
                    w += [f"v_krev{ng-1}_1", f"v_ckv{ng-1}"]
            K.add("tensor", f_mm, waits=w, label=f"t_kr{ng}_{d2}")

        if ng < 2:
            emit_units(tabK[4 * ng:4 * ng + 4])
        else:
            emit_units(tabQ[2 * (ng - 2):2 * (ng - 2) + 2])
        K.add("vector", lambda v, ng=ng: v.tensor_copy(
            ckvS[:].rearrange("p (cc s) -> p cc s", cc=2)[:, :, ng * 512:(ng + 1) * 512],
            pA[:, 1024:2048].rearrange("p (cc s) -> p cc s", cc=2)),
            waits=[f"t_kr{ng}_15"], label=f"v_ckv{ng}")

        for cc in range(2):
            w = [f"t_kr{ng}_15"]
            if (ng, cc) != (0, 0):
                pv = (ng, 0) if cc == 1 else (ng - 1, 1)
                w.append("t_as%d_%d_1" % pv)
            K.add("vector", lambda v, cc=cc: v.tensor_copy(
                rkr[:, 0:512], pA[:, cc * 512:(cc + 1) * 512]),
                waits=w, label=f"v_krev{ng}_{cc}")
            w = [f"v_krev{ng}_{cc}", "c_pk"]
            if (ng, cc) != (0, 0):
                pv = (ng, 0) if cc == 1 else (ng - 1, 1)
                w.append("v_krrope%d_%d" % pv)
            K.add("tensor", lambda t, cc=cc: t.matmul(
                pB[:, 0:512], permKs[:], rkr[:, 0:512], start=True, stop=True),
                waits=w, label=f"t_krsw{ng}_{cc}")
            K.add("vector", lambda v, ng=ng, cc=cc: v.tensor_tensor(
                rop0[:], pB[:, 0:512], sinK[:, ng * 512:(ng + 1) * 512], ALU.mult),
                waits=[f"t_krsw{ng}_{cc}", f"v_tabK{ng}1"], label=f"v_krm1{ng}_{cc}")
            K.add("vector", lambda v, ng=ng, cc=cc: v.tensor_tensor(
                rop1[:], rkr[:, 0:512], cosK[:, ng * 512:(ng + 1) * 512], ALU.mult),
                waits=[f"v_krm1{ng}_{cc}"], label=f"v_krm2{ng}_{cc}")
            K.add("vector", lambda v: v.tensor_tensor(
                rkr[:, 0:512], rop0[:], rop1[:], ALU.add),
                waits=[f"v_krm2{ng}_{cc}"], label=f"v_krrope{ng}_{cc}")
            def f_kc(t, ng=ng, cc=cc):
                ins = None
                for csrc in range(2):
                    ins = t.matmul(pB[:, 512:1024],
                                   upk[:, csrc * 256 + cc * 128: csrc * 256 + (cc + 1) * 128],
                                   ckvS[:, csrc * 2048 + ng * 512: csrc * 2048 + (ng + 1) * 512],
                                   start=(csrc == 0), stop=(csrc == 1))
                return ins
            w = [f"v_ckv{ng}", "c_upk"]
            if (ng, cc) != (0, 0):
                pv = (ng, 0) if cc == 1 else (ng - 1, 1)
                w.append("v_kcev%d_%d" % pv)
            K.add("tensor", f_kc, waits=w, label=f"t_kc{ng}_{cc}")
            w = [f"t_kc{ng}_{cc}"]
            if (ng, cc) != (0, 0):
                pv = (ng, 0) if cc == 1 else (ng - 1, 1)
                w.append("t_as%d_%d_1" % pv)
            K.add("vector", lambda v: v.tensor_copy(kcs[:, 0:512], pB[:, 512:1024]),
                  waits=w, label=f"v_kcev{ng}_{cc}")
            for par in range(2):
                m = 2 * cc + par

                def f_as(t, par=par):
                    t.matmul(pC[:, 0:512], selRs[par][:], rkr[:, 0:512],
                             start=True, stop=False)
                    return t.matmul(pC[:, 0:512], selNs[par][:], kcs[:, 0:512],
                                    start=False, stop=True)
                w = [f"v_krrope{ng}_{cc}", f"v_kcev{ng}_{cc}",
                     f"c_selr{par}", f"c_seln{par}"]
                if (ng, cc, par) == (0, 0, 0):
                    w.append("v_ev15_2")
                else:
                    pv = (ng, cc, 0) if par == 1 else \
                        ((ng, cc - 1, 1) if cc == 1 else (ng - 1, 1, 1))
                    w.append("v_kas%d_%d_%d" % pv)
                K.add("tensor", f_as, waits=w, label=f"t_as{ng}_{cc}_{par}")
                K.add("vector", lambda v, ng=ng, m=m: v.tensor_copy(
                    kT[:, m * 2048 + ng * 512: m * 2048 + (ng + 1) * 512], pC[:, 0:512]),
                    waits=[f"t_as{ng}_{cc}_{par}"], label=f"v_kas{ng}_{cc}_{par}")

    for sc in range(16):
        vps = pD[:, 0:512] if sc % 2 == 0 else pA[:, 0:512]

        def f_v(t, sc=sc, vps=vps):
            ins = None
            for csrc in range(2):
                ins = t.matmul(vps,
                               ckvS[:, csrc * 2048 + sc * 128: csrc * 2048 + (sc + 1) * 128],
                               upv[:, csrc * 512:(csrc + 1) * 512],
                               start=(csrc == 0), stop=(csrc == 1))
            return ins
        w = [f"v_ckv{sc // 4}", "c_upv"]
        if sc >= 2:
            w.append(f"v_vev{sc-2}")
        elif sc == 0:
            w.append("v_ev15_3")
        else:
            w.append("v_krev3_1")
        K.add("tensor", f_v, waits=w, label=f"t_v{sc}")
        K.add("vector", lambda v, sc=sc, vps=vps: v.tensor_copy(
            vT[:, sc * 520:(sc + 1) * 520]
              .rearrange("p (m h c) -> p m h c", h=2, c=65)[:, :, :, 0:64],
            vps.rearrange("p (m h c) -> p m h c", h=2, c=64)),
            waits=[f"t_v{sc}"], label=f"v_vev{sc}")

    if debug:
        K.add("gpsimd", lambda g: g.dma_start(out=dbg["d_k"][:, :], in_=kT[:]),
              waits=["v_kas3_1_1"], label="dbg_k", sem="DO")
        K.add("gpsimd", lambda g: g.dma_start(out=dbg["d_v"][:, :], in_=vT[:]),
              waits=["v_vev15"], label="dbg_v", sem="DO")
        K.add("gpsimd", lambda g: g.dma_start(out=dbg["d_xn"][:, 0:16384], in_=xnP[:]),
              waits=["v_ev15_3", "v_ev15_2"], label="dbg_xn", sem="DO")
        K.add("gpsimd", lambda g: g.dma_start(out=dbg["d_xn"][:, 16384:32768], in_=xnO[:]),
              waits=["v_ev15_3", "v_ev15_2"], label="dbg_xn2", sem="DO")

    _chk("p2")
    s1_cx.close()
    open_stacks.remove(s1_cx)
    S1_LAST = ["v_kas3_1_1", "v_vev15"]
    if debug and not DEBUG_NOGATE:
        S1_LAST.append(("DO", K.cnt["DO"]))
    s2_cx = contextlib.ExitStack()
    open_stacks.append(s2_cx)
    T2 = lambda nm, shp, dt: s2_cx.enter_context(nc.sbuf_tensor(nm, list(shp), dt))
    qT = T2("qT", (128, 16 * OWN), BF16)
    pbuf = [T2(f"pbuf{i}", (128, 1024), BF16) for i in range(4)]
    wq = [T2(f"wq{i}", (128, 2048), BF16) for i in range(3)]
    qsS = T2("qsS", (128, OWN), BF16)
    qswt1 = T2("qswt1", (128, OWN), BF16)
    I128f = T2("I128f", (128, 128), F32)
    dma("sync", I128f[:], I128_d[:, :], label="c_i128f", waits=S1_LAST)
    E2 = T2("E2s", (65, 64), F32)
    SelS = T2("SelSs", (8, 512), F32)
    dnS = T2("dnS", (65, 512), F32)
    maskA = T2("maskAs", (128, 2048), BF16)
    dma("gpsimd", maskA[:], maskAdd_d[:, :], label="c_maska", waits=S1_LAST,
        sem="KRD")
    I8s = T2("I8s", (8, 8), F32)
    RexpS = T2("RexpS", (8, 128), F32)
    dma("sync", I8s[:], I8_d[:, :], label="c_i8", waits=S1_LAST)
    dma("sync", RexpS[:], Rexp_d[:, :], label="c_rexp", waits=S1_LAST)
    rTS = T2("rTS", (128, 8), F32)
    rT8 = T2("rT8", (8, 128), F32)
    dma("sync", E2[:], E2_d[:, :], label="c_e164", waits=S1_LAST)
    dma("sync", SelS[:], SelS_d[:, :], label="c_sels", waits=S1_LAST)
    permQs = T2("permQs", (128, 128), BF16)
    dma("gpsimd", permQs[:], permQ_d[:, :], label="c_pq", waits=S1_LAST, sem="PQ")
    qf32 = [T2(f"qf32_{i}", (128, 2048), F32) for i in range(2)]

    qw_ready = {}
    for qc in range(16):
        w = [f"t_q{qc-3}_mm"] if qc >= 3 else list(S1_LAST)
        k = qc // 2
        if qc % 2 == 1:
            sm = "WQ" if k % 2 == 0 else "WQ2"
            dma("gpsimd", wq[qc % 3][:], qwTt[qc], label=f"dg_wq{qc}", sem=sm,
                waits=w)
            qw_ready[qc] = (sm, 16 * (k // 2 + 1))
        else:
            wf = list(w)
            if qc >= 4:
                wf.append(f"s_qc{qc-4}")
            sm = "QF" if k % 2 == 0 else "QF2"
            dma("sync", qf32[k % 2][:], qwTt[qc], label=f"ds_wqf{qc}",
                sem=sm, waits=wf)
            K.add("scalar", lambda sC, qc=qc, k=k: sC.activation(
                wq[qc % 3][:], qf32[k % 2][:], AF.Copy),
                waits=[(sm, 16 * (k // 2 + 1))] + w, label=f"s_qc{qc}")
            qw_ready[qc] = f"s_qc{qc}"

        qps = pB[:, 0:1024] if qc % 2 == 0 else pA[:, 1024:2048]

        def f_qmm(t, qc=qc, qps=qps):
            ins = None
            for d2 in range(16):
                for ngo in range(2):
                    ins = t.matmul(qps[:, ngo * 512:(ngo + 1) * 512],
                                   wq[qc % 3][:, d2 * 128:(d2 + 1) * 128],
                                   xnO[:, d2 * 1024 + ngo * 512: d2 * 1024 + (ngo + 1) * 512],
                                   start=(d2 == 0), stop=(d2 == 15))
            return ins
        w = [qw_ready[qc]]
        if qc == 0:
            w += S1_LAST
        elif qc == 1:
            w.append("v_ckv3")
        else:
            w.append(f"v_qev{qc-2}")
        K.add("tensor", f_qmm, waits=w, label=f"t_q{qc}_mm")
        w = [f"t_q{qc}_mm"]
        if qc >= 1:
            w.append(f"v_qr2_{qc-1}")
        K.add("vector", lambda v, qps=qps: v.tensor_copy(qsS[:], qps),
              waits=w, label=f"v_qev{qc}")

        def f_qsw(t):
            t.matmul(pA[:, 0:512], permQs[:], qsS[:, 0:512], start=True, stop=True)
            return t.matmul(pA[:, 512:1024], permQs[:], qsS[:, 512:1024],
                            start=True, stop=True)
        w = [f"v_qev{qc}", "c_pq"]
        if qc == 0:
            w += ["v_ckv3", "v_krev3_1", "v_vev15"]
        else:
            w.append(f"v_qr1_{qc-1}")
        K.add("tensor", f_qsw, waits=w, label=f"t_qsw{qc}")
        K.add("vector", lambda v, qc=qc: v.tensor_tensor(
            qT[:, qc * 1024:(qc + 1) * 1024], pA[:, 0:1024], sinQ[:], ALU.mult),
            waits=[f"t_qsw{qc}", tq_done] + ([f"v_qr3_{qc-1}"] if qc >= 1 else []),
            label=f"v_qr1_{qc}")
        K.add("vector", lambda v: v.tensor_tensor(qswt1[:], qsS[:], cosQ[:], ALU.mult),
              waits=[f"v_qr1_{qc}"], label=f"v_qr2_{qc}")
        K.add("vector", lambda v, qc=qc: v.tensor_tensor(
            qT[:, qc * 1024:(qc + 1) * 1024], qT[:, qc * 1024:(qc + 1) * 1024],
            qswt1[:], ALU.add),
            waits=[f"v_qr2_{qc}"], label=f"v_qr3_{qc}")

    if debug:
        K.add("gpsimd", lambda g: g.dma_start(out=dbg["d_q"][:, :], in_=qT[:]),
              waits=["v_qr3_15"], label="dbg_q", sem="DO")

    _chk("p3")
    for _lbl in ("c_e164", "c_sels", "c_i128f", "c_rexp"):
        K.labels[_lbl] = ("SY", K.cnt["SY"])
    gwS = T2("gwS", (128, 128), BF16)
    ATs = T2("ATs", (128, 2048), BF16)
    lgS = T2("lgS", (8, OWN), F32)
    cwS = T2("cwS", (128, 64), F32)
    m1S = T2("m1S", (128, 8), F32)
    m2S = T2("m2S", (128, 8), F32)
    peS = T2("peS", (128, 64), F32)
    keepS = T2("keepS", (128, 64), F32)
    den2S = T2("den2S", (128, 8), F32)
    cwrepS = T2("cwrepS", (128, OWN), F32)
    cwTs = T2("cwTs", (8, OWN), F32)
    lgT = T2("lgT", (128, 64), F32)

    dma("gpsimd", gwS[:], gwTt[:, :], label="c_gw", waits=S1_LAST, sem="GL")
    dma("gpsimd", ATs[:], ATt_d[:, :], label="c_at", waits=S1_LAST, sem="GL")
    for _lbl in ("c_gw", "c_at"):
        K.labels[_lbl] = ("GL", K.cnt["GL"])

    def f_lg(t):
        ins = None
        for d2 in range(16):
            for ngo in range(2):
                ins = t.matmul(pB[0:8, ngo * 512:(ngo + 1) * 512],
                               gwS[:, d2 * 8:(d2 + 1) * 8],
                               xnO[:, d2 * 1024 + ngo * 512: d2 * 1024 + (ngo + 1) * 512],
                               start=(d2 == 0), stop=(d2 == 15))
        return ins
    K.add("tensor", f_lg, waits=["c_gw", "v_qev14"], label="t_lg")
    K.add("vector", lambda v: v.tensor_copy(lgS[:], pB[0:8, 0:1024]),
          waits=["t_lg"], label="v_lgev")
    _chk("g1")

    def f_lgt(t):
        ins = None
        for tc in range(8):
            ins = t.matmul(pC[:, 8 * tc:8 * tc + 8], lgS[:, tc * 128:(tc + 1) * 128],
                           I128f[0:8, 0:8], start=True, stop=True)
        return ins
    K.add("tensor", f_lgt, waits=["v_lgev", "c_i128f", "v_kas3_1_1"], label="t_lgt")
    K.add("vector", lambda v: v.tensor_copy(lgT[:], pC[:, 0:64]),
          waits=["t_lgt"], label="v_lgt")
    lg3 = lgT[:].rearrange("p (tc e) -> p tc e", e=8)
    K.add("vector", lambda v: v.reduce_max(m1S[:], lg3, axis=AX.X, negate=True),
          waits=["v_lgt"], label="v_m1")
    _chk("g2")

    def f_pe(s):
        ins = None
        for tc in range(8):
            ins = s.activation(peS[:, 8 * tc:8 * tc + 8], lgT[:, 8 * tc:8 * tc + 8],
                               AF.Exp, bias=m1S[:, tc:tc + 1])
        return ins
    K.add("scalar", f_pe, waits=["v_m1"], label="s_pe")
    _chk("g3")

    def f_eq(v):
        ins = None
        for tc in range(8):
            ins = v.tensor_scalar(keepS[:, 8 * tc:8 * tc + 8], lgT[:, 8 * tc:8 * tc + 8],
                                  m1S[:, tc:tc + 1], None, ALU.add)
        return ins
    K.add("vector", f_eq, waits=["v_m1"], label="v_eq0")
    K.add("vector", lambda v: v.tensor_single_scalar(keepS[:], keepS[:], 0.0, ALU.is_ge),
          waits=["v_eq0"], label="v_eq1")
    _chk("g3a")
    K.add("vector", lambda v: v.scalar_tensor_tensor(
        keepS[:], keepS[:], -1e30, lgT[:], ALU.mult, ALU.add),
        waits=["v_eq1"], label="v_p2")
    _chk("g3b")
    K.add("vector", lambda v: v.reduce_max(
        m2S[:], keepS[:].rearrange("p (tc e) -> p tc e", e=8), axis=AX.X),
        waits=["v_p2"], label="v_m2")
    _chk("g3c")

    def f_keep(v):
        ins = None
        for tc in range(8):
            ins = v.tensor_scalar(keepS[:, 8 * tc:8 * tc + 8], lgT[:, 8 * tc:8 * tc + 8],
                                  m2S[:, tc:tc + 1], None, ALU.subtract)
        return ins
    K.add("vector", f_keep, waits=["v_m2"], label="v_kp0")
    K.add("vector", lambda v: v.tensor_single_scalar(keepS[:], keepS[:], 0.0, ALU.is_ge),
          waits=["v_kp0"], label="v_kp1")
    K.add("vector", lambda v: v.tensor_tensor(peS[:], peS[:], keepS[:], ALU.mult),
          waits=["v_kp1", "s_pe"], label="v_kp2")
    _chk("g3d")
    K.add("vector", lambda v: v.reduce_sum(
        den2S[:], peS[:].rearrange("p (tc e) -> p tc e", e=8), axis=AX.X),
        waits=["v_kp2"], label="v_den2")
    K.add("vector", lambda v: v.reciprocal(m1S[:], den2S[:]),
          waits=["v_den2"], label="v_rcd")
    _chk("g3e")

    def f_cw(v):
        ins = None
        for tc in range(8):
            ins = v.tensor_scalar(cwS[:, 8 * tc:8 * tc + 8], peS[:, 8 * tc:8 * tc + 8],
                                  m1S[:, tc:tc + 1], None, ALU.mult)
        return ins
    K.add("vector", f_cw, waits=["v_rcd"], label="v_cw")
    _chk("g4")
    if debug:
        K.add("gpsimd", lambda g: g.dma_start(out=dbg["d_cw"][:, :], in_=cwS[:]),
              waits=["v_cw"], label="dbg_cw", sem="DO")

    for tc in range(8):
        w = ["v_cw", "c_i128f"]
        w.append("v_vev15" if tc == 0 else f"v_cwtev{tc-1}")
        K.add("tensor", lambda t, tc=tc: t.matmul(
            pD[0:8, 0:128], cwS[:, 8 * tc:8 * tc + 8], I128f[:], start=True, stop=True),
            waits=w, label=f"t_cwt{tc}")
        K.add("vector", lambda v, tc=tc: v.tensor_copy(
            cwTs[:, tc * 128:(tc + 1) * 128], pD[0:8, 0:128]),
            waits=[f"t_cwt{tc}"], label=f"v_cwtev{tc}")
        if tc == 7:
            _chk("g5")

    def f_cwr(t):
        ins = None
        for ngo in range(2):
            ins = t.matmul(pB[:, ngo * 512:(ngo + 1) * 512], RexpS[:],
                           cwTs[:, ngo * 512:(ngo + 1) * 512], start=True, stop=True)
        return ins
    K.add("tensor", f_cwr, waits=["v_cwtev7", "c_rexp", "v_lgev"], label="t_cwr")
    K.add("vector", lambda v: v.tensor_copy(cwrepS[:], pB[:, 0:1024]),
          waits=["t_cwr"], label="v_cwrev")
    _chk("g6")

    def f_ta(t):
        ins = None
        for d2 in range(16):
            for ngo in range(2):
                ins = t.matmul(pB[:, ngo * 512:(ngo + 1) * 512],
                               ATs[:, d2 * 128:(d2 + 1) * 128],
                               xnO[:, d2 * 1024 + ngo * 512: d2 * 1024 + (ngo + 1) * 512],
                               start=(d2 == 0), stop=(d2 == 15))
        return ins
    K.add("tensor", f_ta, waits=["c_at", "v_cwrev"], label="t_ta")
    K.add("vector", lambda v: v.tensor_tensor(tsS[:], pB[:, 0:1024], cwrepS[:], ALU.mult),
          waits=["t_ta", "v_cwrev"], label="v_ts")
    if debug:
        K.add("gpsimd", lambda g: g.dma_start(out=dbg["d_ts"][:, :], in_=tsS[:]),
              waits=["v_ts"], label="dbg_ts", sem="DO")


    kci = 0
    g_idx = 0
    last_exp = {0: None, 1: None}
    last_pv4 = {i: None for i in range(4)}
    last_repev = {0: None, 1: None}
    prev_norm = None
    prev_repev = None
    pending_stages = []

    def _XA(p, r0, r1, c0, c1):
        return pC[r0:r1, c0:c1] if p == 0 else pB[r0:r1, c0:c1]

    def _YA(p, r0, r1, c0, c1):
        return pD[r0:r1, c0:c1] if p == 0 else pB[r0:r1, 512 + c0:512 + c1]

    for hp in range(16):
        m = hp // 4
        for qb in (0, 1):
            q0 = qb * 512
            nd = 4 if qb == 0 else 12
            dg0 = 8 if qb == 0 else 12
            chunks = [(kc, False) for kc in range(nd)] + \
                     [(dg0 + i, True) for i in range(4)]
            nch = len(chunks)
            kci_base = kci
            gp = g_idx % 2

            def emit_sc(ci, kc, par, isdiag, hp=hp, m=m, q0=q0, kci_base=kci_base):
                def f_sc(t, kc=kc, par=par, m=m, q0=q0, hp=hp, isdiag=isdiag):
                    qtop = qT[0:64, hp * 1024 + q0: hp * 1024 + q0 + 512]
                    qbot = qT[64:128, hp * 1024 + q0: hp * 1024 + q0 + 512]
                    i = kc % 4
                    t.matmul(pA[:, par * 1024: par * 1024 + 512],
                             kT[0:64, m * 2048 + kc * 128: m * 2048 + (kc + 1) * 128],
                             qtop, start=True, stop=not isdiag)
                    ins = t.matmul(pA[:, par * 1024 + 512: par * 1024 + 1024],
                                   kT[64:128, m * 2048 + kc * 128: m * 2048 + (kc + 1) * 128],
                                   qbot, start=True, stop=not isdiag,
                                   tile_position=(64, 0))
                    if isdiag:
                        t.matmul(pA[:, par * 1024: par * 1024 + 512],
                                 I128[:], maskA[:, 512 * i:512 * i + 512],
                                 start=False, stop=True)
                        ins = t.matmul(pA[:, par * 1024 + 512: par * 1024 + 1024],
                                       I128[:], maskA[:, 512 * i:512 * i + 512],
                                       start=False, stop=True)
                    return ins
                w = []
                kci0 = kci_base + ci
                if kci0 == 0:
                    w += ["v_qr3_15", "v_qr1_15", "c_maska", "c_i128"]
                if kci0 == 1:
                    w += ["v_qev15"]
                if last_exp[par]:
                    w.append(last_exp[par])
                K.add("tensor", f_sc, waits=w, label=f"t_sc{kci0}")

            def emit_rest(ci, kc, isdiag, par, first, lastc, kci_base=kci_base,
                          m=m, qb=qb, gp=gp, g_idx=g_idx):
                kci0 = kci_base + ci
                slot = kci0 % 4

                def f_ex(sC, par=par, kc=kc, slot=slot, qb=qb):
                    return sC.activation(pbuf[slot][:], pA[:, par * 1024:(par + 1) * 1024],
                                         AF.Exp, scale=csc[:, CC_125:CC_125 + 1],
                                         bias=biasK[:, qb * 16 + kc: qb * 16 + kc + 1])
                w = [f"t_sc{kci0}", "c_bk"]
                if last_pv4[slot]:
                    w.append(last_pv4[slot])
                K.add("scalar", f_ex, waits=w, label=f"s_ex{kci0}")
                last_exp[par] = f"s_ex{kci0}"
                pv_src = f"s_ex{kci0}"
                def f_pv(t, kc=kc, slot=slot, first=first, lastc=lastc, m=m, gp=gp):
                    c0 = kc * 520 + 130 * m
                    t.matmul(_XA(gp, 0, 65, 0, 512), vT[:, c0:c0 + 65],
                             pbuf[slot][:, 0:512], start=first, stop=lastc,
                             tile_position=(0, 0))
                    t.matmul(_YA(gp, 64, 128, 0, 512), vT[:, c0 + 65:c0 + 129],
                             pbuf[slot][:, 512:1024], start=first, stop=lastc,
                             tile_position=(0, 64))
                    return t.matmul(_YA(gp, 0, 1, 0, 512), onesb[:],
                                    pbuf[slot][:, 512:1024],
                                    start=first, stop=lastc, tile_position=(0, 0))
                w = [pv_src, "c_ones"]
                if first:
                    if last_repev[gp]:
                        w.append(last_repev[gp])
                    if kci0 == 0:
                        w += ["v_vev15", "v_qev15", "v_lgt", "v_cwtev7"]
                    if g_idx == 1:
                        w += ["v_qev15", "v_ts"]
                K.add("tensor", f_pv, waits=w, label=f"t_pv{kci0}")
                last_pv4[slot] = f"t_pv{kci0}"

            emit_sc(0, chunks[0][0], kci % 2, chunks[0][1])
            emit_sc(1, chunks[1][0], (kci + 1) % 2, chunks[1][1])
            stages = pending_stages
            pending_stages = []
            for ci, (kc, isdiag) in enumerate(chunks):
                par = (kci_base + ci) % 2
                emit_rest(ci, kc, isdiag, par, ci == 0, ci == nch - 1)
                if ci < len(stages):
                    stages[ci]()
                if ci + 2 < nch:
                    emit_sc(ci + 2, chunks[ci + 2][0], (kci_base + ci + 2) % 2,
                            chunks[ci + 2][1])
                kci += 1

            lab = f"{hp}_{qb}"
            c0a = hp * 1024 + qb * 512
            kl = kci - 1
            bp = gp

            def s_dnraw(lab=lab, c0a=c0a, kl=kl, bp=bp):
                def f_dnr(v):
                    v.tensor_copy(dnS[64:65, 0:512], _XA(bp, 64, 65, 0, 512))
                    v.tensor_copy(dnS[0:1, 0:512], _YA(bp, 0, 1, 0, 512))
                    v.tensor_copy(attnO[0:64, c0a:c0a + 512], _XA(bp, 0, 64, 0, 512))
                    return v.tensor_copy(attnO[64:128, c0a:c0a + 512],
                                         _YA(bp, 64, 128, 0, 512))
                K.add("vector", f_dnr, waits=[f"t_pv{kl}"], label=f"v_dnr{lab}")

            def s_r1(lab=lab, bp=bp):
                def f_r1(t):
                    ins = None
                    for j in range(8):
                        rr = 64 if j < 4 else 0
                        sidx = j % 4
                        src = dnS[rr:rr + 1, 128 * sidx: 128 * sidx + 128]
                        ins = t.matmul(_XA(bp, 0, 128, 504, 512), src,
                                       E2[rr:rr + 1, 8 * j:8 * j + 8],
                                       start=(j == 0), stop=(j == 7),
                                       tile_position=(rr, 0))
                    return ins
                K.add("tensor", f_r1, waits=[f"v_dnr{lab}", "c_e164"],
                      label=f"t_r1_{lab}")

            def s_rts(lab=lab, bp=bp):
                K.add("vector",
                      lambda v: v.reciprocal(rTS[:], _XA(bp, 0, 128, 504, 512)),
                      waits=[f"t_r1_{lab}"], label=f"v_rts{lab}")

            def s_rt8(lab=lab, bp=bp):
                K.add("tensor",
                      lambda t: t.matmul(_XA(bp, 0, 8, 376, 504), rTS[:], I128f[:],
                                         start=True, stop=True),
                      waits=[f"v_rts{lab}", "c_i128f"], label=f"t_rt8_{lab}")

            def s_rt8ev(lab=lab, bp=bp):
                K.add("vector",
                      lambda v: v.tensor_copy(rT8[:], _XA(bp, 0, 8, 376, 504)),
                      waits=[f"t_rt8_{lab}"], label=f"v_rt8_{lab}")

            def s_rep(lab=lab, bp=bp):
                def f_rep(t):
                    ins = None
                    for sidx in range(4):
                        ins = t.matmul(_XA(bp, 0, 128, 128 * sidx, 128 * sidx + 128),
                                       SelS[:, 128 * sidx:128 * sidx + 128],
                                       rT8[:], start=True, stop=True)
                    return ins
                K.add("tensor", f_rep, waits=[f"v_rt8_{lab}", "c_sels"],
                      label=f"t_rep{lab}")

            def s_repev(lab=lab, bp=bp):
                K.add("vector",
                      lambda v: v.tensor_copy(repS[:], _XA(bp, 0, 128, 0, 512)),
                      waits=[f"t_rep{lab}"], label=f"v_repev{lab}")

            def s_nm(lab=lab, c0a=c0a):
                def f_nm(v):
                    v.tensor_tensor(attnO[0:64, c0a:c0a + 512],
                                    attnO[0:64, c0a:c0a + 512],
                                    repS[0:64, :], ALU.mult)
                    return v.tensor_tensor(attnO[64:128, c0a:c0a + 512],
                                           attnO[64:128, c0a:c0a + 512],
                                           repS[64:128, :], ALU.mult)
                K.add("vector", f_nm, waits=[f"v_repev{lab}"], label=f"v_nm{lab}")

            pending_stages = [s_dnraw, s_r1, s_rts, s_rt8, s_rt8ev,
                              s_rep, s_repev, s_nm]
            last_repev[gp] = f"v_repev{lab}"
            prev_repev = f"v_repev{lab}"
            prev_norm = f"v_nm{lab}"
            g_idx += 1
    for st_ in pending_stages:
        st_()
    pending_stages = []

    if debug:
        K.add("gpsimd", lambda g: g.dma_start(out=dbg["d_at"][:, :], in_=attnO[:]),
              waits=["v_nm15_1"], label="dbg_at", sem="DO")

    _chk("p4")
    _chk("p5pre")
    s2_cx.close()
    kv_cx.close()
    open_stacks.remove(s2_cx)
    open_stacks.remove(kv_cx)

    s3_cx = contextlib.ExitStack()
    open_stacks.append(s3_cx)
    T3 = lambda nm, shp, dt: s3_cx.enter_context(nc.sbuf_tensor(nm, list(shp), dt))
    BTs = T3("BTs", (128, 2048), BF16)
    fwS = [T3(f"fwS{i}", (128, 8192), BF16) for i in range(2)]
    owS = [T3(f"owS{i}", (128, 8192), BF16) for i in range(2)]
    gS = T3("gS", (128, 512), BF16)
    sgS = T3("sgS", (128, 512), F32)
    xoS = [T3(f"xoS{i}", (128, 512), F32) for i in range(2)]
    yS = [T3(f"yS{i}", (128, 512), F32) for i in range(2)]

    ydst = y_out.rearrange("(n p) d -> n p d", p=128)
    xo_src = x_in.rearrange("(n p) d -> n p d", p=128)

    S2_LAST = ["v_ts", "v_nm15_1"]
    if debug and not DEBUG_NOGATE:
        S2_LAST.append(("DO", K.cnt["DO"]))
    dma("gpsimd", BTs[:], BTt_d[:, :], label="c_bt", waits=list(S2_LAST), sem="GL")
    for og in range(4):
        w = list(S2_LAST) if og == 0 else [f"t_o{og-1}_7"]
        dma("gpsimd", fwS[og % 2][:], fwTt[og], label=f"dg_fw{og}", waits=w, sem="SLAB")
        dma("gpsimd", owS[og % 2][:], owTt[og], label=f"dg_ow{og}", waits=w, sem="SLAB2")
        for tc in range(8):
            it = og * 8 + tc
            par = it % 2
            if it >= 2:
                w = [f"v_y{og}_{tc-2}"] if tc >= 2 else [f"v_y{og-1}_{6+tc}"]
            else:
                w = list(S2_LAST)
            dma("sync", xoS[par][:], xo_src[8 + tc][:, og * 512:(og + 1) * 512],
                label=f"ds_xo{og}_{tc}", waits=w, sem="XO")

            def f_fd(t, og=og, tc=tc, par=par):
                for d2 in range(16):
                    t.matmul(pB[:, par * 512:(par + 1) * 512],
                             xnO[:, d2 * 1024 + tc * 128: d2 * 1024 + (tc + 1) * 128],
                             fwS[og % 2][:, d2 * 512:(d2 + 1) * 512],
                             start=(d2 == 0), stop=False)
                return t.matmul(pB[:, par * 512:(par + 1) * 512],
                                tsS[:, tc * 128:(tc + 1) * 128],
                                BTs[:, og * 512:(og + 1) * 512],
                                start=False, stop=True)
            w = [("SLAB", 16 * (og + 1)), "c_bt", "v_ts"]
            if it >= 2:
                w.append(f"s_g{og}_{tc-2}" if tc >= 2 else f"s_g{og-1}_{6+tc}")
            else:
                w.append("v_repev15_1")
            K.add("tensor", f_fd, waits=w, label=f"t_fd{og}_{tc}")

            def f_o(t, og=og, tc=tc, par=par):
                ins = None
                for c2 in range(16):
                    ins = t.matmul(pA[:, par * 512:(par + 1) * 512],
                                   attnO[:, c2 * 1024 + tc * 128: c2 * 1024 + (tc + 1) * 128],
                                   owS[og % 2][:, c2 * 512:(c2 + 1) * 512],
                                   start=(c2 == 0), stop=(c2 == 15))
                return ins
            w = [("SLAB2", 16 * (og + 1))]
            if it >= 2:
                w.append(f"v_sg{og}_{tc-2}" if tc >= 2 else f"v_sg{og-1}_{6+tc}")
            else:
                w += ["s_ex383", "v_nm15_1"]
            K.add("tensor", f_o, waits=w, label=f"t_o{og}_{tc}")

            def f_g(s, og=og, par=par):
                return s.activation(gS[:], pB[:, par * 512:(par + 1) * 512], AF.Silu)
            w = [f"t_fd{og}_{tc}"]
            if it >= 1:
                w.append(f"v_sg{og}_{tc-1}" if tc >= 1 else f"v_sg{og-1}_7")
            K.add("scalar", f_g, waits=w, label=f"s_g{og}_{tc}")

            def f_sg(v, par=par):
                return v.tensor_tensor(sgS[:], pA[:, par * 512:(par + 1) * 512],
                                       gS[:], ALU.mult)
            w = [f"s_g{og}_{tc}", f"t_o{og}_{tc}"]
            if it >= 1:
                w.append(f"v_y{og}_{tc-1}" if tc >= 1 else f"v_y{og-1}_7")
            K.add("vector", f_sg, waits=w, label=f"v_sg{og}_{tc}")

            def f_y(v, par=par):
                return v.tensor_tensor(yS[par][:], sgS[:], xoS[par][:], ALU.add)
            w = [f"v_sg{og}_{tc}", ("XO", 16 * (it + 1))]
            if it >= 2:
                w.append(("YD", 16 * it))
            K.add("vector", f_y, waits=w, label=f"v_y{og}_{tc}")
            dma("sync", ydst[tc][:, og * 512:(og + 1) * 512], yS[par][:],
                label=f"ds_y{og}_{tc}", waits=[f"v_y{og}_{tc}"], sem="YD")

    n_yd = K.cnt["YD"]
    K.raw("sync", lambda e: e.wait_ge(sems["YD"], n_yd))
    if debug:
        n_do = K.cnt["DO"]
        K.raw("sync", lambda e: e.wait_ge(sems["DO"], n_do))
    if STOP_AFTER is not None:
        mk = _marks[STOP_AFTER]
        for e in K.steps:
            K.steps[e] = K.steps[e][:mk[e]]
        wsem, wval = K.labels[_stop_label[STOP_AFTER]]
        K.raw("sync", lambda e: e.wait_ge(sems[wsem], wval))
    _emit_and_close(nc, K, sems, list(reversed(open_stacks)))
    return nc


def _emit_and_close(nc, K, sems, stacks):
    with nc.Block() as block:
        @block.tensor
        def _(t):
            K.emit("tensor", t, sems)

        @block.vector
        def _(v):
            K.emit("vector", v, sems)

        @block.scalar
        def _(s):
            K.emit("scalar", s, sems)

        @block.gpsimd
        def _(g):
            K.emit("gpsimd", g, sems)

        @block.sync
        def _(sy):
            K.emit("sync", sy, sems)
    for st_ in stacks:
        st_.close()




_NC_CACHE = {}


def kernel(**inputs):
    in_maps = _prep_inputs(inputs)
    key = ("main", DEBUG)
    if key not in _NC_CACHE:
        _NC_CACHE[key] = build_nc(debug=DEBUG)
    nc = _NC_CACHE[key]
    res = run_bass_kernel_spmd(nc, in_maps, core_ids=list(range(NCORES)))
    y = np.empty((4, S, D), np.float32)
    for c in range(NCORES):
        b, h = divmod(c, 2)
        blocks = ([0, 3] if h == 0 else [1, 2])
        yc = res.results[c]["y"]
        for i, p in enumerate(blocks):
            y[b, 512 * p:512 * (p + 1), :] = yc[512 * i:512 * (i + 1)]
    return y
